# version 22
# baseline (speedup 1.0000x reference)
import os
import numpy as np
import concourse.bass as bass
import concourse.mybir as mybir
from concourse.bass_utils import run_bass_kernel_spmd

F32 = mybir.dt.float32
BF16 = mybir.dt.bfloat16
AF = mybir.ActivationFunctionType
ALU = mybir.AluOpType

H = 16
DH = 64
DM = 1024
DF = 4096
NCORES = 8
EPS_LN = 1e-5


class Em:

    ENGINES = ("tensor", "vector", "scalar", "sync", "gpsimd")

    TOTAL_FAMS = ("c_gp", "c_sync")

    def __init__(self):
        self.ops = []
        self.res_w = {}
        self.res_r = {}
        self.last_pe = None
        self.forced = {}

    def op(self, eng, fn, r=(), w=(), dma=False, fam=None, pe_sync=False,
           after=()):
        idx = len(self.ops)
        assert (fam is not None) == dma
        deps = set()
        if pe_sync and self.last_pe is not None:
            deps.add(self.last_pe)
            self.forced.setdefault(idx, set()).add(self.last_pe)
        for a in after:
            deps.add(a)
            self.forced.setdefault(idx, set()).add(a)
        for res in r:
            lw = self.res_w.get(res)
            if lw is not None:
                deps.add(lw)
        for res in w:
            lw = self.res_w.get(res)
            if lw is not None:
                deps.add(lw)
            deps.update(self.res_r.get(res, ()))
        for res in r:
            self.res_r.setdefault(res, []).append(idx)
        for res in w:
            self.res_w[res] = idx
            self.res_r[res] = []
        if eng == "tensor" and fam is None:
            self.last_pe = idx
        self.ops.append((eng, fn, deps, fam))
        return idx

    DUR = {"tensor": 160, "vector": 450, "scalar": 550, "sync": 1500, "gpsimd": 1500}
    LAT = 400

    def schedule(self):
        import heapq
        ops = self.ops
        n = len(ops)
        succ = [[] for _ in range(n)]
        ndep = [0] * n
        for i, (eng, fn, deps, fam) in enumerate(ops):
            ndep[i] = len(deps)
            for d in deps:
                succ[d].append(i)
        prio = [0] * n
        for i in range(n - 1, -1, -1):
            eng = ops[i][0]
            d = self.DUR[eng]
            best = 0
            for sx in succ[i]:
                v = prio[sx] + (0 if ops[sx][0] == eng else self.LAT)
                if v > best:
                    best = v
            prio[i] = best + d
        for i, (eng, fn, deps, fam) in enumerate(ops):
            if fam in self.TOTAL_FAMS:
                prio[i] += 10 ** 12
        etime = {e: 0 for e in self.ENGINES}
        fin = [0] * n
        ready = []
        cnt = 0
        for i in range(n):
            if ndep[i] == 0:
                heapq.heappush(ready, (-prio[i], i))
        order = {e: [] for e in self.ENGINES}
        scheduled = 0
        def rtime(i, eng):
            rt = 0
            for d in ops[i][2]:
                t = fin[d] + (0 if ops[d][0] == eng else self.LAT)
                if t > rt:
                    rt = t
            return rt

        while ready:
            cand = [heapq.heappop(ready)]
            for _ in range(7):
                if not ready:
                    break
                cand.append(heapq.heappop(ready))
            best_k, best_st = 0, None
            for k, (negp, j) in enumerate(cand):
                e2 = ops[j][0]
                st = max(etime[e2], rtime(j, e2))
                if best_st is None or st < best_st - 100:
                    best_k, best_st = k, st
            negp, i = cand.pop(best_k)
            for c in cand:
                heapq.heappush(ready, c)
            eng, fn, deps, fam = ops[i]
            start = max(etime[eng], rtime(i, eng))
            fin[i] = start + self.DUR[eng]
            etime[eng] = fin[i]
            order[eng].append(i)
            scheduled += 1
            for sx in succ[i]:
                ndep[sx] -= 1
                if ndep[sx] == 0:
                    heapq.heappush(ready, (-prio[sx], sx))
        assert scheduled == len(ops)
        return order

    def finalize(self, nc, block):
        ops = self.ops
        order = self.schedule()
        pos = {}
        for e in self.ENGINES:
            for k, i in enumerate(order[e]):
                pos[i] = k
        fam_ord = {}
        fam_cnt = {}
        for e in self.ENGINES:
            for i in order[e]:
                fam = ops[i][3]
                if fam is not None:
                    fam_ord[i] = fam_cnt.get(fam, 0)
                    fam_cnt[fam] = fam_ord[i] + 1
        red = {}
        for i, (eng, fn, deps, fam) in enumerate(ops):
            best = {}
            for d in deps:
                deng, _, _, dfam = ops[d]
                key = ("d", dfam) if dfam is not None else ("e", deng)
                if deng == eng == "tensor" and dfam is None:
                    if d in self.forced.get(i, ()):
                        key = ("f", "tensor")
                    else:
                        continue
                b = best.get(key)
                if b is None or pos[d] > pos[b]:
                    best[key] = d
            red[i] = list(best.values())
        marked = set()
        for i, (eng, fn, deps, fam) in enumerate(ops):
            for d in red[i]:
                deng, _, _, dfam = ops[d]
                if dfam is None:
                    if not (deng == eng == "tensor") or d in self.forced.get(i, ()):
                        marked.add(d)
        mark_ord = {}
        cnt = {e: 0 for e in self.ENGINES}
        per_eng = order
        for e in self.ENGINES:
            for i in per_eng[e]:
                if i in marked:
                    cnt[e] += 1
                    mark_ord[i] = cnt[e]
        self._sem_ctx = []
        sems = {}
        fsems = {}
        for e in self.ENGINES:
            cm = nc.semaphore(f"es_{e}")
            self._sem_ctx.append(cm)
            sems[e] = cm.__enter__()
        for fam in fam_cnt:
            cm = nc.semaphore(f"df_{fam}")
            self._sem_ctx.append(cm)
            fsems[fam] = cm.__enter__()

        def body_for(ename):
            def body(engine):
                wm = {}
                for i in per_eng[ename]:
                    eng, fn, deps, fam = ops[i]
                    need = {}
                    for d in red[i]:
                        deng, _, _, dfam = ops[d]
                        if dfam is not None:
                            key = ("d", dfam)
                            if dfam in self.TOTAL_FAMS or str(dfam).startswith("dbg"):
                                val = 16 * fam_cnt[dfam]
                            else:
                                val = 16 * (fam_ord[d] + 1)
                        else:
                            if deng == ename == "tensor" and d not in self.forced.get(i, ()):
                                continue
                            key = ("e", deng)
                            val = mark_ord[d]
                        if need.get(key, -1) < val:
                            need[key] = val
                    for key in sorted(need, key=str):
                        val = need[key]
                        if wm.get(key, -1) >= val:
                            continue
                        wm[key] = val
                        engine.wait_ge((fsems[key[1]] if key[0] == "d" else sems[key[1]]), val)
                    inst = fn(engine)
                    if fam is not None:
                        inst.then_inc(fsems[fam], 16)
                    elif i in marked:
                        inst.then_inc(sems[ename], 1)
            return body

        for ename in self.ENGINES:
            getattr(block, ename)(body_for(ename))


def host_constants():
    c = {}
    ar = np.arange(64)
    a128 = np.arange(128) % 64
    c["c_id32"] = np.eye(128, dtype=np.float32)
    c["c_idbf"] = np.eye(128, dtype=np.float32)
    c["c_maskun"] = (ar[None, :] < a128[:, None]) * np.float32(1)
    c["c_maskpr"] = (ar[None, :] > a128[:, None]) * np.float32(1)
    c["c_maskpri"] = (ar[None, :] >= a128[:, None]) * np.float32(1)
    ind = np.zeros((16, 16, 64), np.float32)
    for j in range(16):
        ind[j, j, :] = 1.0
    c["c_ind64"] = np.concatenate([ind.reshape(16, 1024),
                                   np.zeros((112, 1024), np.float32)], 0)
    c["c_nonel"] = np.concatenate([np.tile(np.where(np.arange(128) < 64, -1.0, 0.0)
                                           .astype(np.float32), (16, 1)),
                                   np.zeros((112, 128), np.float32)], 0)
    c["c_noneh"] = np.concatenate([np.tile(np.where(np.arange(128) >= 64, -1.0, 0.0)
                                           .astype(np.float32), (16, 1)),
                                   np.zeros((112, 128), np.float32)], 0)
    b16 = np.zeros((16, 8, 128), np.float32)
    for blk in range(8):
        b16[2 * blk, blk, 0:64] = 1.0
        b16[2 * blk + 1, blk, 64:128] = 1.0
    c["c_b16"] = np.concatenate([b16.reshape(16, 1024),
                                 np.zeros((112, 1024), np.float32)], 0)
    sq = np.zeros((128, 8, 16), np.float32)
    for blk in range(8):
        sq[0:64, blk, 2 * blk] = 1.0
        sq[64:128, blk, 2 * blk + 1] = 1.0
    c["c_sq16"] = sq.reshape(128, 128)
    par = np.zeros((16, 128), np.float32)
    for m in range(16):
        par[m, 0:64] = 1.0 if m % 2 == 0 else 0.0
        par[m, 64:128] = 0.0 if m % 2 == 0 else 1.0
    c["c_parity"] = np.concatenate([par, np.zeros((112, 128), np.float32)], 0)
    pr = np.zeros((16, 8, 8), np.float32)
    for m in range(16):
        pr[m, :, m // 2] = 1.0
    c["c_pairrep"] = np.concatenate([pr.reshape(16, 64),
                                     np.zeros((112, 64), np.float32)], 0)
    c["c_onesrow"] = np.concatenate([np.ones((1, 128), np.float32),
                                     np.zeros((127, 128), np.float32)], 0)
    c["c_onescol"] = np.ones((128, 1), np.float32)
    return c

BF_CONSTS = ["c_idbf", "c_maskun", "c_maskpr", "c_maskpri", "c_ind64",
             "c_nonel", "c_noneh", "c_b16", "c_sq16", "c_parity",
             "c_pairrep", "c_onescol"]
F32_CONSTS = ["c_id32", "c_onesrow"]

WEIGHTS = [("wqT", DM, DM), ("wkT", DM, DM), ("wvT", DM, DM), ("wgT", DM, DM),
           ("woT", DM, DM), ("w1T", DM, DF), ("w2T", DM, DF),
           ("h1T", DF, DF), ("h2T", DF, DF), ("w3T", DF, DM)]


SKIP = set(os.environ.get("KSKIP", "").split(",")) - {""}
KDUMP = os.environ.get("KDUMP", "") == "1"


def build_program(T=8192, CT=512):
    nc = bass.Bass("TRN2", target_bir_lowering=False)
    NTILE = T // CT
    NDBL = CT // 128
    cc = host_constants()

    xt = nc.dram_tensor("xt", [DM, T], F32, kind="ExternalInput").ap()
    yt = nc.dram_tensor("yt", [DM, T], F32, kind="ExternalOutput").ap()
    w_in = {nm: nc.dram_tensor(nm, [a, b], F32, kind="ExternalInput").ap()
            for nm, a, b in WEIGHTS}
    w_in["wab"] = nc.dram_tensor("wab", [DM, 32], F32, kind="ExternalInput").ap()
    sc_in = {nm: nc.dram_tensor(nm, list(shp), F32, kind="ExternalInput").ap()
             for nm, shp in [("s1s", (128, 32)), ("s2s", (128, 32)),
                             ("hs1", (128, 32)), ("hs2", (128, 32)),
                             ("w3s", (128, 8)), ("bacol", (16, CT)), ("bbcol", (16, CT))]}
    cst_in = {nm: nc.dram_tensor(nm, list(cc[nm].shape), F32, kind="ExternalInput").ap()
              for nm in BF_CONSTS + F32_CONSTS}
    scr = {nm.replace("T", "S"): nc.dram_tensor(nm.replace("T", "S"), [a, b], BF16).ap()
           for nm, a, b in WEIGHTS}
    dbg = {}
    if KDUMP:
        for nm in ("d_xn", "d_qn", "d_kn", "d_vb", "d_gate", "d_og", "d_ao", "d_ybf",
                   "d_yn", "d_ST"):
            dbg[nm] = nc.dram_tensor(nm, [128, 8 * CT], BF16, kind="ExternalOutput").ap()

    def dump(nm, t8, res8):
        if not KDUMP:
            return
        O("sync", (lambda e: e.dma_start(dbg[nm][:, :], t8[:, 0:8 * CT])),
          r=tuple((res8, b) for b in range(8)) if res8 else (), w=("dbgdone",),
          dma=True, fam="dbg" + nm)

    em = Em()
    _ctx = []

    def sb(name, shape, dt):
        t = nc.sbuf_tensor(name, shape, dt)
        _ctx.append(t)
        return t.__enter__()

    def psum(name, shape, dt):
        t = nc.psum_tensor(name, shape, dt)
        _ctx.append(t)
        return t.__enter__()

    K = {}
    for nm in BF_CONSTS:
        K[nm] = sb("sb_" + nm, list(cc[nm].shape), BF16)
    for nm in F32_CONSTS:
        K[nm] = sb("sb_" + nm, list(cc[nm].shape), F32)
    S = {nm: sb("sb_" + nm, list(ap.shape), F32) for nm, ap in sc_in.items()}
    wabsb = sb("wabsb", [128, 256], BF16)
    wslot = [sb(f"wslot{i}", [128, 2048], BF16) for i in range(3)]
    ST = sb("ST", [128, 512], F32)
    STbf = sb("STbf", [128, 512], BF16)
    stsc = sb("stsc", [128, 512], F32)
    outb = [sb(f"outb{i}", [128, CT], F32) for i in range(2)]

    xbf = sb("xbf", [128, 8 * CT], BF16)
    xn = sb("xn", [128, 8 * CT], BF16)
    qn = sb("qn", [128, 8 * CT], BF16)
    qg = sb("qg", [128, 8 * CT], BF16)
    kn = sb("kn", [128, 8 * CT], BF16)
    kb = sb("kb", [128, 8 * CT], BF16)
    kgb = sb("kgb", [128, 8 * CT], BF16)
    vb = sb("vb", [128, 8 * CT], BF16)
    gate = sb("gate", [128, 8 * CT], BF16)

    sqscr = sb("sqscr", [128, CT], BF16)
    sqscr2 = sb("sqscr2", [128, CT], BF16)
    mbs = sb("mbs", [128, CT], BF16)
    rbs = sb("rbs", [128, CT], BF16)
    rowsA = sb("rowsA", [128, 2 * CT], F32)
    rowsB = sb("rowsB", [128, 2 * CT], F32)
    rowf1 = sb("rowf1", [128, 5 * CT], F32)
    rowf2 = sb("rowf2", [16, 5 * CT], F32)
    rG = (rowf1[0:16, 0:CT], ("rf1", 0))
    rgv = (rowf1[0:16, CT:2 * CT], ("rf1", 1))
    rgr = (rowf1[0:16, 2 * CT:3 * CT], ("rf1", 2))
    rbr = (rowf1[0:16, 3 * CT:4 * CT], ("rf1", 3))
    rla = (rowf1[0:16, 4 * CT:5 * CT], ("rf1", 4))
    rgp = rla
    r2 = [(rowf2[:, i * CT:(i + 1) * CT], ("rf2", i)) for i in range(5)]
    rowbA = sb("rowbA", [128, 4 * CT], BF16)
    rowbB = sb("rowbB", [128, 2 * CT], BF16)
    rowg = sb("rowg", [128, 2 * CT], BF16)
    mov2 = [sb("mov2_0", [128, 1024], BF16)]
    gC = sb("gC", [128, 64], F32)
    SMTb = sb("SMTb", [128, 16], BF16)
    epsb = sb("epsb", [1, 1], F32)

    escr = sb("escr", [128, 1024], F32)
    Eun = sb("Eun", [128, 1024], BF16)
    Eprs = sb("Eprs", [128, 1024], BF16)
    Epri = sb("Epri", [128, 1024], BF16)
    Aun_s = sb("Aun_s", [128, 1024], BF16)
    Aprs = sb("Aprs", [128, 1024], BF16)
    Akqs = sb("Akqs", [128, 1024], BF16)
    P0s = sb("P0s", [128, 1024], BF16)
    Xp = [{m: sb(f"Xp{m}_{i}", [128, 1024], BF16) for m in (1, 2, 4, 8, 16, 32)}
          for i in range(2)]
    Xua = sb("Xua", [128, 1024], BF16)
    Xub = sb("Xub", [128, 1024], BF16)
    Zs = [sb(f"Zs{i}", [128, 1024], BF16) for i in range(2)]
    Wgt = sb("Wgt", [128, 1024], BF16)
    VbTs = sb("VbTs", [128, 1024], BF16)
    Ktoks = sb("Ktoks", [128, 1024], BF16)
    P0s2 = [sb(f"P0s{i}", [128, 1024], BF16) for i in range(2)]
    SMTb2 = [sb(f"SMTb{i}", [128, 16], BF16) for i in range(2)]

    pA = psum("pA", [128, 1024], F32)
    pB = psum("pB", [128, 1024], F32)
    pC = psum("pC", [128, 1024], F32)
    pS = psum("pS", [128, 512], F32)
    ptr = psum("ptr", [128, 1024], BF16)
    RA = ("pA0", "pA1")
    RB = ("pB0", "pB1")
    RC = ("pC0", "pC1")
    PSETS = [
        [(pA[:, 0:CT], ("pA0",)), (pA[:, 512:512 + CT], ("pA1",))],
        [(pB[:, 0:CT], ("pB0",)), (pB[:, 512:512 + CT], ("pB1",))],
        [(pC[:, 0:CT], ("pC0",)), (pC[:, 512:512 + CT], ("pC1",))],
    ]

    O = em.op

    def blk(t8, b):
        return t8[:, b * CT:(b + 1) * CT]

    def mk_map(pairs):
        out = []
        for t8, rname in pairs:
            nb = t8.shape[1] // CT
            for b in range(nb):
                res = (rname, b) if rname in ("kn", "kb", "kgb", "vb", "gate", "xbf") else rname
                out.append((t8[:, b * CT:(b + 1) * CT], res))
        return out

    u1map = mk_map([(kn, "kn"), (kb, "kb"), (kgb, "kgb"), (vb, "vb")])
    h2map = mk_map([(gate, "gate"), (xbf, "xbf"),
                    (Xp[0][1], "Xp1_0"), (Xp[0][2], "Xp2_0"),
                    (Xp[0][4], "Xp4_0"), (Xp[0][8], "Xp8_0"),
                    (Xp[1][1], "Xp1_1"), (Xp[1][2], "Xp2_1"),
                    (Xp[1][4], "Xp4_1"), (Xp[1][8], "Xp8_1"),
                    (Zs[0], "Zs0"), (Zs[1], "Zs1")])[:32]

    for nm in BF_CONSTS:
        O("gpsimd", (lambda e, nm=nm: e.dma_start(K[nm][:, :], cst_in[nm][:, :])),
          w=(nm,), dma=True, fam="c_gp")
    for nm in F32_CONSTS:
        O("sync", (lambda e, nm=nm: e.dma_start(K[nm][:, :], cst_in[nm][:, :])),
          w=(nm,), dma=True, fam="c_sync")
    for nm in S:
        O("sync", (lambda e, nm=nm: e.dma_start(S[nm][:, :], sc_in[nm][:, :])),
          w=(nm,), dma=True, fam="c_sync")
    O("gpsimd", lambda e: e.dma_start(
        wabsb[:, :].rearrange("p (j c) -> p j c", j=8),
        w_in["wab"][:, :].rearrange("(j p) c -> p j c", p=128)),
      w=("wabsb",), dma=True, fam="c_gp")
    O("vector", lambda e: e.memset(ST[:, :], 0.0), w=("ST",))
    _pads = [(rowf1, [("rf1", i) for i in range(5)]),
             (rowbA, [(("rf2", 0), "bf"), (("rf2", 2), "bf"),
                      (("rf2", 1), "bf"), (("rf2", 3), "bf")]),
             (rowbB, [(("rf2", 4), "bf"), (("rf1", 3), "bf")]),
             (rowg, ["Ghib", "Glob"]), (mov2[0], [("m2", 0)]),
             (rowsA, ["rA0", "rA1"]), (rowsB, ["rB0", "rB1"])]
    for t_, rl in _pads:
        O("vector", (lambda e, t_=t_: e.memset(t_[:, :], 0.0)), w=tuple(rl))
    O("vector", lambda e: e.memset(STbf[:, :], 0.0), w=("STbf",))
    O("vector", lambda e: e.memset(epsb[:, :], EPS_LN), w=("epsb",))

    SLOTW = 8 * CT
    ppbufs = [(gate, tuple(("gate", b) for b in range(8))),
              (xbf, tuple(("xbf", b) for b in range(8)))]
    ppi = 0
    for nm, din, dout in WEIGHTS:
        dst = nm.replace("T", "S")
        for rr in range(din // 128):
            for c0 in range(0, dout, SLOTW):
                cw = min(SLOTW, dout - c0)
                buf, bres = ppbufs[ppi % 2]
                ppi += 1
                pfam = "g" if ppi % 2 == 1 else "x"
                O("gpsimd", (lambda e, nm=nm, rr=rr, c0=c0, cw=cw, buf=buf:
                             e.dma_start(buf[:, 0:cw],
                                         w_in[nm][128 * rr:128 * rr + 128, c0:c0 + cw])),
                  w=bres, dma=True, fam="ppg" + pfam)
                O("sync", (lambda e, dst=dst, rr=rr, c0=c0, cw=cw, buf=buf:
                           e.dma_start(scr[dst][128 * rr:128 * rr + 128, c0:c0 + cw],
                                       buf[:, 0:cw])),
                  r=bres, w=((dst, rr),), dma=True, fam="ppo" + pfam)

    def ln_stats(src_t, src_res):
        for b in range(8):
            sq = (sqscr, sqscr2)[b % 2]
            sqr = ("sqscr", "sqscr2")[b % 2]
            O("gpsimd", (lambda e, b=b, sq=sq: e.tensor_mul(sq[:, :], blk(src_t, b), blk(src_t, b))),
              r=((src_res, b),), w=(sqr,))
            O("tensor", (lambda e, b=b: e.matmul(pS[0:1, 0:CT], K["c_onescol"][:, :],
                                                 blk(src_t, b), start=(b == 0), stop=(b == 7))),
              r=((src_res, b), "c_onescol"), w=("pS",))
            O("tensor", (lambda e, b=b, sq=sq: e.matmul(pC[0:1, 0:CT], K["c_onescol"][:, :],
                                                 sq[:, :], start=(b == 0), stop=(b == 7))),
              r=(sqr, "c_onescol"), w=RC)
        O("vector", lambda e: e.tensor_single_scalar(rowsA[0:1, 0:CT], pS[0:1, 0:CT],
                                                     1.0 / DM, ALU.mult),
          r=("pS",), w=("rA0",))
        O("vector", lambda e: e.tensor_mul(rowsB[0:1, 0:CT], rowsA[0:1, 0:CT], rowsA[0:1, 0:CT]),
          r=("rA0",), w=("rB0",))
        O("vector", lambda e: e.scalar_tensor_tensor(rowsB[0:1, CT:2 * CT], pC[0:1, 0:CT],
                                                     1.0 / DM, rowsB[0:1, 0:CT],
                                                     ALU.mult, ALU.subtract),
          r=RC + ("rB0",), w=("rB1",))
        O("scalar", lambda e: e.activation(rowsA[0:1, CT:2 * CT], rowsB[0:1, CT:2 * CT],
                                           AF.Ln, bias=epsb[:, :]),
          r=("rB1", "epsb"), w=("rA1",))
        O("scalar", lambda e: e.activation(rowsB[0:1, 0:CT], rowsA[0:1, CT:2 * CT],
                                           AF.Exp, scale=-0.5),
          r=("rA1",), w=("rB0",))
        O("tensor", lambda e: e.matmul(pS[0:128, 0:CT], K["c_onesrow"][:, :],
                                       rowsA[0:128, 0:CT], start=True, stop=True),
          r=("rA0", "c_onesrow"), w=("pS",))
        O("tensor", lambda e: e.matmul(pC[0:128, 0:CT], K["c_onesrow"][:, :],
                                       rowsB[0:128, 0:CT], start=True, stop=True),
          r=("rB0", "c_onesrow"), w=RC)
        O("vector", lambda e: e.tensor_copy(mbs[:, :], pS[0:128, 0:CT]), r=("pS",), w=("mbs",))
        O("vector", lambda e: e.tensor_copy(rbs[:, :], pC[0:128, 0:CT]), r=RC, w=("rbs",))

    def ln_apply(src_t, src_res, dst_t, dst_res, resid_t=None, resid_res=None):
        for b in range(8):
            scr2 = sqscr2
            O("vector", (lambda e, b=b: e.tensor_sub(scr2[:, :], blk(src_t, b), mbs[:, :])),
              r=((src_res, b), "mbs"), w=("sqscr2",))
            if resid_t is None:
                O("vector", (lambda e, b=b: e.tensor_mul(blk(dst_t, b), scr2[:, :], rbs[:, :])),
                  r=("sqscr2", "rbs"), w=((dst_res, b),))
            else:
                O("vector", (lambda e, b=b: e.tensor_mul(scr2[:, :], scr2[:, :], rbs[:, :])),
                  r=("sqscr2", "rbs"), w=("sqscr2",))
                O("vector", (lambda e, b=b: e.tensor_add(blk(dst_t, b), scr2[:, :],
                                                         blk(resid_t, b))),
                  r=("sqscr2", (resid_res, b)), w=((dst_res, b),))

    wsl = [0]

    def stream_mm(wS, din, dout, pmaps4, mov_ap, mov_res, epi):
        nob = dout // 128
        nsb = din // 512
        for pi, og0 in enumerate(range(0, nob, 4)):
            pmaps = pmaps4
            width = 512
            for sbk in range(nsb):
                slot = wslot[wsl[0] % 3]
                sres = f"wslot{wsl[0] % 3}"
                wsl[0] += 1
                O("sync", (lambda e, sbk=sbk, og0=og0, slot=slot, width=width, wS=wS:
                           e.dma_start(slot[:, 0:4 * width].rearrange("p (jj o) -> p jj o", jj=4),
                                       scr[wS][512 * sbk:512 * sbk + 512,
                                               128 * og0:128 * og0 + width]
                                       .rearrange("(jj p) o -> p jj o", p=128))),
                  r=tuple((wS, 4 * sbk + j) for j in range(4)), w=(sres,), dma=True, fam=sres)
                for jj in range(4):
                    j = 4 * sbk + jj
                    for oi in range(4):
                        pap, bres = pmaps[oi]
                        stat = slot[:, jj * width + oi * 128: jj * width + (oi + 1) * 128]
                        O("tensor", (lambda e, pap=pap, stat=stat, mv=mov_ap(j),
                                     first=(sbk == 0 and jj == 0),
                                     last=(sbk == nsb - 1 and jj == 3):
                                     e.matmul(pap, stat, mv, start=first, stop=last)),
                          r=(sres, mov_res(j)), w=bres)
            for oi in range(4):
                epi(og0 + oi, *pmaps[oi])

    def pmap4():
        return [(pA[:, 0:CT], ("pA0",)), (pA[:, 512:512 + CT], ("pA1",)),
                (pB[:, 0:CT], ("pB0",)), (pB[:, 512:512 + CT], ("pB1",))]

    def logsig(psrc, psrc_res, bias, dst, dst_res):
        (zap, zr), (sap, sr), (map_, mr) = r2[0], r2[1], r2[2]
        O("vector", (lambda e: e.tensor_copy(zap, psrc)),
          r=(psrc_res,), w=(zr,))
        O("vector", (lambda e: e.tensor_add(zap, zap, S[bias][:, :])),
          r=(zr, bias), w=(zr,))
        O("scalar", lambda e: e.activation(sap, zap, AF.Abs),
          r=(zr,), w=(sr,))
        O("scalar", lambda e: e.activation(sap, sap, AF.Exp, scale=-1.0),
          r=(sr,), w=(sr,))
        O("scalar", lambda e: e.activation(sap, sap, AF.Ln, bias=1.0),
          r=(sr,), w=(sr,))
        O("vector", lambda e: e.tensor_single_scalar(map_, zap, 0.0, ALU.min),
          r=(zr,), w=(mr,))
        O("vector", (lambda e: e.tensor_sub(dst, map_, sap)),
          r=(mr, sr), w=(dst_res,))

    seg = CT // 64

    for ti in range(NTILE):
        ts = ti * CT
        O("gpsimd", (lambda e, ts=ts: e.dma_start(
            xbf[:, :].rearrange("p (b t) -> p b t", b=8),
            xt[:, ts:ts + CT].rearrange("(b p) t -> p b t", p=128))),
          w=tuple(("xbf", b) for b in range(8)), dma=True, fam="xin")

        ln_stats(xbf, "xbf")
        ln_apply(xbf, "xbf", xn, "xn")
        if ti == 0:
            dump("d_xn", xn, "xn")

        def epi_copy(dst_t, dres):
            def f(o, pap, bres):
                O("scalar", (lambda e, o=o, pap=pap: e.copy(blk(dst_t, o), pap)),
                  r=bres, w=((dres, o),))
            return f

        def epi_silu(dst_t, dres):
            def f(o, pap, bres):
                O("scalar", (lambda e, o=o, pap=pap: e.activation(blk(dst_t, o), pap, AF.Silu)),
                  r=bres, w=((dres, o),))
            return f

        mov_xn = lambda j: blk(xn, j)
        res_xn = lambda j: ("xn", j)
        stream_mm("wqS", DM, DM, pmap4(), mov_xn, res_xn, epi_copy(qg, "qg"))
        stream_mm("wkS", DM, DM, pmap4(), mov_xn, res_xn, epi_copy(kn, "kn"))
        stream_mm("wvS", DM, DM, pmap4(), mov_xn, res_xn, epi_copy(vb, "vb"))
        stream_mm("wgS", DM, DM, pmap4(), mov_xn, res_xn, epi_silu(gate, "gate"))

        for j in range(8):
            O("tensor", (lambda e, j=j: e.matmul(pA[0:16, 0:CT], wabsb[:, j * 32:j * 32 + 16],
                                                 blk(xn, j), start=(j == 0), stop=(j == 7))),
              r=(("xn", j), "wabsb"), w=("pA0",))
            O("tensor", (lambda e, j=j: e.matmul(pB[0:16, 0:CT],
                                                 wabsb[:, j * 32 + 16:j * 32 + 32],
                                                 blk(xn, j), start=(j == 0), stop=(j == 7))),
              r=(("xn", j), "wabsb"), w=("pB0",))

        logsig(pA[0:16, 0:CT], "pA0", "bacol", rla[0], rla[1])
        O("vector", lambda e: e.tensor_single_scalar(rla[0], rla[0], -60.0, ALU.max),
          r=(rla[1],), w=(rla[1],))
        logsig(pB[0:16, 0:CT], "pB0", "bbcol", r2[3][0], r2[3][1])
        O("scalar", lambda e: e.activation(rbr[0], r2[3][0], AF.Exp),
          r=(r2[3][1],), w=(rbr[1],))
        O("vector", lambda e: e.tensor_copy(rG[0], rla[0]),
          r=(rla[1],), w=(rG[1],))
        for sh in (1, 2, 4, 8, 16, 32):
            O("vector", (lambda e, sh=sh: e.tensor_add(
                rgp[0].rearrange("p (s c) -> p s c", c=64)[:, :, sh:64],
                rG[0].rearrange("p (s c) -> p s c", c=64)[:, :, sh:64],
                rG[0].rearrange("p (s c) -> p s c", c=64)[:, :, 0:64 - sh])),
              r=(rG[1],), w=(rgp[1],))
            O("vector", (lambda e, sh=sh: e.tensor_copy(
                rgp[0].rearrange("p (s c) -> p s c", c=64)[:, :, 0:sh],
                rG[0].rearrange("p (s c) -> p s c", c=64)[:, :, 0:sh])),
              r=(rG[1],), w=(rgp[1],))
            O("vector", lambda e: e.tensor_copy(rG[0], rgp[0]),
              r=(rgp[1],), w=(rG[1],))
        O("scalar", lambda e: e.activation(rgv[0], rG[0], AF.Exp),
          r=(rG[1],), w=(rgv[1],))
        gend = rG[0].rearrange("p (s c) -> p s c", c=64)[:, :, 63:64] \
            .broadcast_to([16, seg, 64])
        O("vector", lambda e: e.tensor_sub(
            rgr[0].rearrange("p (s c) -> p s c", c=64),
            rG[0].rearrange("p (s c) -> p s c", c=64), gend),
          r=(rG[1],), w=(rgr[1],))
        O("scalar", lambda e: e.activation(rgr[0], rgr[0], AF.Exp, scale=-1.0),
          r=(rgr[1],), w=(rgr[1],))
        O("vector", lambda e: e.tensor_copy(rowg[0:16, 0:CT], rG[0]),
          r=(rG[1],), w=("Ghib",))
        O("vector", lambda e: e.tensor_sub(rowg[0:16, CT:2 * CT], rG[0],
                                           rowg[0:16, 0:CT]),
          r=(rG[1], "Ghib"), w=("Glob",))
        gvend = rgv[0].rearrange("p (s c) -> p s c", c=64)[:, :, 63:64] \
            .broadcast_to([16, seg, 8])
        O("vector", (lambda e: e.tensor_mul(
            mov2[0][0:16, 0:8 * seg].rearrange("p (s c) -> p s c", c=8), gvend,
            K["c_pairrep"][0:16, 0:8 * seg].rearrange("p (s c) -> p s c", c=8))),
          r=(rgv[1], "c_pairrep"), w=(("m2", 0),))
        O("tensor", lambda e: e.matmul(pS[0:128, 0:8 * seg], K["c_parity"][:, :],
                                       mov2[0][:, 0:8 * seg], start=True, stop=True),
          r=(("m2", 0), "c_parity"), w=("pS",))
        O("vector", lambda e: e.tensor_copy(gC[:, 0:8 * seg], pS[0:128, 0:8 * seg]),
          r=("pS",), w=("gC",))

        for raw_t, raw_res, pap, bres, dst, dres in (
                (qg, "qg", pS[0:16, 0:CT], ("pS",), r2[0][0], r2[0][1]),
                (kn, "kn", pC[0:16, 0:CT], RC, r2[1][0], r2[1][1])):
            for b in range(8):
                sq = (sqscr, sqscr2)[b % 2]
                sqr = ("sqscr", "sqscr2")[b % 2]
                O("gpsimd", (lambda e, b=b, raw_t=raw_t, sq=sq: e.tensor_mul(
                    sq[:, :], blk(raw_t, b), blk(raw_t, b))),
                  r=((raw_res, b),), w=(sqr,))
                O("tensor", (lambda e, b=b, pap=pap, sq=sq: e.matmul(
                    pap, K["c_sq16"][:, b * 16:b * 16 + 16], sq[:, :],
                    start=(b == 0), stop=(b == 7))),
                  r=(sqr, "c_sq16"), w=bres)
            O("scalar", (lambda e, pap=pap, dst=dst: e.activation(dst, pap, AF.Ln)),
              r=bres, w=(dres,))
            O("scalar", (lambda e, dst=dst: e.activation(dst, dst, AF.Exp, scale=-0.5)),
              r=(dres,), w=(dres,))
            O("vector", (lambda e, dst=dst: e.tensor_single_scalar(dst, dst, 1e6, ALU.min)),
              r=(dres,), w=(dres,))
        O("vector", lambda e: e.tensor_mul(r2[2][0], r2[0][0], rgv[0]),
          r=(r2[0][1], rgv[1]), w=(r2[2][1],))
        O("vector", lambda e: e.scalar_tensor_tensor(r2[3][0], r2[1][0],
                                                     -1.0, rbr[0],
                                                     ALU.mult, ALU.mult),
          r=(r2[1][1], rbr[1]), w=(r2[3][1],))
        O("vector", lambda e: e.tensor_mul(r2[4][0], r2[3][0], rgv[0]),
          r=(r2[3][1], rgv[1]), w=(r2[4][1],))
        rowsrc = [(r2[0][1], r2[0][0], rowbA[0:16, 0:CT]),
                  (r2[2][1], r2[2][0], rowbA[0:16, CT:2 * CT]),
                  (r2[1][1], r2[1][0], rowbA[0:16, 2 * CT:3 * CT]),
                  (r2[3][1], r2[3][0], rowbA[0:16, 3 * CT:4 * CT]),
                  (r2[4][1], r2[4][0], rowbB[0:16, 0:CT]),
                  (rbr[1], rbr[0], rowbB[0:16, CT:2 * CT])]
        for res, src, dst in rowsrc:
            O("vector", (lambda e, src=src, dst=dst: e.tensor_copy(dst, src)),
              r=(res,), w=((res, "bf"),))
        rowap = {}
        rowres = {}
        fullv = [rowbA[:, 0:CT], rowbA[:, CT:2 * CT], rowbA[:, 2 * CT:3 * CT],
                 rowbA[:, 3 * CT:4 * CT], rowbB[:, 0:CT], rowbB[:, CT:2 * CT]]
        for (nm, (res, _, dst)), fv in zip(
                zip(("rnq", "rg", "rnk", "rkb", "rkg", "brow"), rowsrc), fullv):
            rowap[nm] = fv
            rowres[nm] = (res, "bf")

        for b in range(8):
            def bc(res, pap, bres):
                O("tensor", (lambda e, res=res, pap=pap, b=b: e.matmul(
                    pap, K["c_b16"][:, b * 128:(b + 1) * 128], rowap[res],
                    start=True, stop=True)),
                  r=(rowres[res], "c_b16"), w=bres)
            bc("rnq", pS[0:128, 0:CT], ("pS",))
            O("vector", (lambda e, b=b: e.scalar_tensor_tensor(
                blk(qn, b), pS[0:128, 0:CT], 1.0, blk(qg, b), ALU.mult, ALU.mult)),
              r=("pS", ("qg", b)), w=(("qn", b),))
            bc("rg", pC[0:128, 0:CT], RC)
            O("vector", (lambda e, b=b: e.scalar_tensor_tensor(
                blk(qg, b), pC[0:128, 0:CT], 1.0, blk(qg, b), ALU.mult, ALU.mult)),
              r=RC + (("qg", b),), w=(("qg", b),))
            bc("rkb", pS[0:128, 0:CT], ("pS",))
            O("vector", (lambda e, b=b: e.scalar_tensor_tensor(
                blk(kb, b), pS[0:128, 0:CT], 1.0, blk(kn, b), ALU.mult, ALU.mult)),
              r=("pS", ("kn", b)), w=(("kb", b),))
            bc("rkg", pC[0:128, 0:CT], RC)
            O("vector", (lambda e, b=b: e.scalar_tensor_tensor(
                blk(kgb, b), pC[0:128, 0:CT], 1.0, blk(kn, b), ALU.mult, ALU.mult)),
              r=RC + (("kn", b),), w=(("kgb", b),))
            bc("rnk", pS[0:128, 0:CT], ("pS",))
            O("vector", (lambda e, b=b: e.scalar_tensor_tensor(
                blk(kn, b), pS[0:128, 0:CT], 1.0, blk(kn, b), ALU.mult, ALU.mult)),
              r=("pS", ("kn", b)), w=(("kn", b),))
            bc("brow", pC[0:128, 0:CT], RC)
            O("vector", (lambda e, b=b: e.scalar_tensor_tensor(
                blk(vb, b), pC[0:128, 0:CT], 1.0, blk(vb, b), ALU.mult, ALU.mult)),
              r=RC + (("vb", b),), w=(("vb", b),))

        if ti == 0:
            dump("d_qn", qn, "qn")
            dump("d_kn", kn, "kn")
            dump("d_vb", vb, "vb")
            dump("d_gate", gate, "gate")

        def gen_SF(d, bi):
            t0 = d * 128
            XpB, P0sB, SMTbB = Xp[bi], P0s2[bi], SMTb2[bi]
            rs = lambda nm: f"{nm}_{bi}"

            def hsl(t8, h, c):
                return blk(t8, h // 2)[64 * (h % 2):64 * (h % 2) + 64,
                                       t0 + 64 * c: t0 + 64 * c + 64]

            O("tensor", (lambda e, t0=t0: e.transpose(
                pS[0:128, 0:128], rowf1[:, 2 * CT + t0:2 * CT + t0 + 128],
                K["c_id32"][:, :])),
              r=(rgr[1], "c_id32"), w=("pS",))
            O("vector", (lambda e, SMTbB=SMTbB: e.tensor_copy(SMTbB[:, 0:16], pS[0:128, 0:16])),
              r=("pS",), w=(rs("SMTb"),))
            for g2 in range(2):
                for rres, rap in (("Ghib", rowg[:, t0:t0 + 128]),
                                  ("Glob", rowg[:, CT + t0:CT + t0 + 128])):
                    O("tensor", (lambda e, g2=g2, rap=rap, first=(rres == "Ghib"):
                                 e.matmul(pA[:, 512 * g2:512 * g2 + 512], rap,
                                          K["c_ind64"][:, 512 * g2:512 * g2 + 512],
                                          start=first, stop=False)),
                      r=(rres, "c_ind64"), w=(RA[g2],))
            yield
            for half, nmsk in ((0, "c_nonel"), (1, "c_noneh")):
                for hl, rres in ((0, "Ghib"), (1, "Glob")):
                    c0r = (0, CT)[hl]
                    src = rowg[0:16, c0r + t0 + 64 * half: c0r + t0 + 64 * half + 64] \
                        .unsqueeze(1).broadcast_to([16, 16, 64])
                    O("vector", (lambda e, src=src: e.tensor_mul(
                        mov2[0][0:16, :].rearrange("p (h c) -> p h c", c=64),
                        K["c_ind64"][0:16, :].rearrange("p (h c) -> p h c", c=64), src)),
                      r=(rres, "c_ind64"), w=(("m2", 0),))
                    for g2 in range(2):
                        last = (half == 1 and hl == 1)
                        O("tensor", (lambda e, g2=g2, nmsk=nmsk, last=last:
                                     e.matmul(pA[:, 512 * g2:512 * g2 + 512], K[nmsk][:, :],
                                              mov2[0][:, 512 * g2:512 * g2 + 512],
                                              start=False, stop=last)),
                          r=(("m2", 0), nmsk), w=(RA[g2],))
            yield
            O("vector", lambda e: e.tensor_single_scalar(escr[:, :], pA[:, :], 0.0, ALU.min),
              r=RA, w=("escr",))
            O("scalar", lambda e: e.activation(Eun[:, :], escr[:, :], AF.Exp),
              r=("escr",), w=("Eun",))
            O("vector", lambda e: e.tensor_scalar(escr[:, :], pA[:, :], -1.0, 0.0,
                                                  ALU.mult, ALU.min),
              r=RA, w=("escr",))
            O("scalar", lambda e: e.activation(Eprs[:, :], escr[:, :], AF.Exp),
              r=("escr",), w=("Eprs",))
            O("vector", lambda e: e.tensor_mul(
                Epri[:, :].rearrange("p (h c) -> p h c", c=64),
                Eprs[:, :].rearrange("p (h c) -> p h c", c=64),
                K["c_maskpri"][:, :].unsqueeze(1).broadcast_to([128, 16, 64])),
              r=("Eprs", "c_maskpri"), w=("Epri",))
            O("vector", lambda e: e.tensor_mul(
                Eprs[:, :].rearrange("p (h c) -> p h c", c=64),
                Eprs[:, :].rearrange("p (h c) -> p h c", c=64),
                K["c_maskpr"][:, :].unsqueeze(1).broadcast_to([128, 16, 64])),
              r=("Eprs", "c_maskpr"), w=("Eprs",))
            O("vector", lambda e: e.tensor_mul(
                Eun[:, :].rearrange("p (h c) -> p h c", c=64),
                Eun[:, :].rearrange("p (h c) -> p h c", c=64),
                K["c_maskun"][:, :].unsqueeze(1).broadcast_to([128, 16, 64])),
              r=("Eun", "c_maskun"), w=("Eun",))
            yield
            ablocks = [(0, 0), (0, 64), (64, 64), (64, 0)]
            for akind, (s1f, s2f, evt, evr) in enumerate((
                    (kb, kn, Aun_s, "Aun_s"), (kn, kb, Aprs, "Aprs"),
                    (kn, qn, Akqs, "Akqs"))):
                s1r = ("kb", "kn", "kn")[akind]
                s2r = ("kn", "kb", "qn")[akind]
                row0_ids = []
                for bi2, (hb, cb) in enumerate(ablocks):
                    c = cb // 64
                    for h in range(hb // 64, H, 2):
                        tp = (hb, cb) if (hb or cb) else None
                        oid = O("tensor", (lambda e, sA=hsl(s1f, h, c), m=hsl(s2f, h, c),
                                           h=h, cb=cb, tp=tp:
                                           e.matmul(pA[cb:cb + 64, 64 * h:64 * h + 64], sA, m,
                                                    start=True, stop=True, tile_position=tp)),
                              r=((s1r, h // 2), (s2r, h // 2)),
                              w=(RA[0],) if h < 8 else (RA[1],),
                              after=tuple(row0_ids) if bi2 >= 2 else ())
                        if bi2 < 2:
                            row0_ids.append(oid)
                ev_eng = ("vector", "scalar", "vector")[akind]
                fn = (lambda e, evt=evt: e.tensor_copy(evt[:, :], pA[:, :])) \
                    if ev_eng == "vector" else \
                    (lambda e, evt=evt: e.copy(evt[:, :], pA[:, :]))
                O(ev_eng, fn, r=RA, w=(evr,))
                yield
            O("vector", (lambda e, XpB=XpB: e.tensor_mul(XpB[1][:, :], Aprs[:, :], Eprs[:, :])),
              r=("Aprs", "Eprs"), w=(rs("Xp1"),))
            O("vector", lambda e: e.tensor_mul(Xua[:, :], Aun_s[:, :], Eun[:, :]),
              r=("Aun_s", "Eun"), w=("Xua",))
            O("vector", (lambda e, P0sB=P0sB: e.tensor_mul(P0sB[:, :], Akqs[:, :], Epri[:, :])),
              r=("Akqs", "Epri"), w=(rs("P0s"),))
            yield
            prevp, prevpt = rs("Xp1"), XpB[1]
            prevu, prevut = "Xua", Xua
            for m in (2, 4, 8, 16, 32):
                for h in range(H):
                    for c in range(2):
                        cb = 64 * c
                        tp = (cb, cb) if cb else None
                        O("tensor", (lambda e, h=h, cb=cb, tp=tp, su=prevut, sp=prevpt:
                                     e.matmul(pA[cb:cb + 64, 64 * h:64 * h + 64],
                                              su[cb:cb + 64, 64 * h:64 * h + 64],
                                              sp[cb:cb + 64, 64 * h:64 * h + 64],
                                              start=True, stop=True, tile_position=tp)),
                          r=(prevu, prevp), w=(RA[0],) if h < 8 else (RA[1],))
                O("vector", (lambda e, m=m, XpB=XpB: e.tensor_copy(XpB[m][:, :], pA[:, :])),
                  r=RA, w=(rs(f"Xp{m}"),))
                yield
                if m != 32:
                    for h in range(H):
                        for c in range(2):
                            cb = 64 * c
                            tp = (cb, cb) if cb else None
                            O("tensor", (lambda e, h=h, cb=cb, tp=tp, su=prevut, sp=prevpt:
                                         e.matmul(pA[cb:cb + 64, 64 * h:64 * h + 64],
                                                  sp[cb:cb + 64, 64 * h:64 * h + 64],
                                                  su[cb:cb + 64, 64 * h:64 * h + 64],
                                                  start=True, stop=True, tile_position=tp)),
                              r=(prevu, prevp), w=(RA[0],) if h < 8 else (RA[1],))
                    nxt = Xub if prevut is Xua else Xua
                    nxtr = "Xub" if prevut is Xua else "Xua"
                    O("scalar", (lambda e, nxt=nxt: e.copy(nxt[:, :], pA[:, :])),
                      r=RA, w=(nxtr,))
                    prevu, prevut = nxtr, nxt
                    yield
                prevp, prevpt = rs(f"Xp{m}"), XpB[m]

        def gen_SEQ(d, bi):
            t0 = d * 128
            XpB, P0sB, SMTbB = Xp[bi], P0s2[bi], SMTb2[bi]
            VbTsB, KtoksB = VbTs, Ktoks
            rs = lambda nm: f"{nm}_{bi}"

            def hsl(t8, h, c):
                return blk(t8, h // 2)[64 * (h % 2):64 * (h % 2) + 64,
                                       t0 + 64 * c: t0 + 64 * c + 64]

            for src_t, sres, dst, dres in ((vb, "vb", VbTsB, "VbTs"),
                                           (kn, "kn", KtoksB, "Ktoks")):
                for pr in range(8):
                    for c in range(2):
                        cb = 64 * c
                        src = blk(src_t, pr)[:, t0 + 64 * c: t0 + 64 * c + 64]
                        O("tensor", (lambda e, src=src, pr=pr, cb=cb: e.transpose(
                            ptr[cb:cb + 64, 128 * pr:128 * pr + 128], src,
                            K["c_idbf"][:, :],
                            tile_position=(0, cb) if cb else None)),
                          r=((sres, pr), "c_idbf"), w=("ptr",))
                O("vector", (lambda e, dst=dst: e.tensor_copy(dst[:, :], ptr[:, :])),
                  r=("ptr",), w=(dres,))
                yield
            for c in range(2):
                cb = 64 * c
                cg = 2 * d + c
                prev_ids = []
                for g2 in range(2):
                    oid = O("tensor", (lambda e, cb=cb, g2=g2, VbTsB=VbTsB: e.matmul(
                        pB[cb:cb + 64, 512 * g2:512 * g2 + 512],
                        K["c_idbf"][cb:cb + 64, cb:cb + 64],
                        VbTsB[cb:cb + 64, 512 * g2:512 * g2 + 512],
                        start=True, stop=False, skip_group_check=True,
                        tile_position=(cb, cb) if cb else None)),
                      r=("VbTs", "c_idbf"), w=(RB[g2],))
                    prev_ids.append(oid)
                for par in (0, 1):
                    cur_ids = []
                    for h in range(par, H, 2):
                        hb = 64 * par
                        pr = h // 2
                        oid = O("tensor", (lambda e, sA=hsl(kgb, h, c), hb=hb, pr=pr, cb=cb, h=h:
                                     e.matmul(pB[cb:cb + 64, 64 * h:64 * h + 64], sA,
                                              STbf[hb:hb + 64, 64 * pr:64 * pr + 64],
                                              start=False, stop=False, skip_group_check=True,
                                              tile_position=(hb, cb) if (hb or cb) else None)),
                          r=(("kgb", h // 2), "STbf"), w=(RB[0],) if h < 8 else (RB[1],),
                          after=tuple(prev_ids))
                        cur_ids.append(oid)
                    prev_ids = cur_ids
                O("vector", (lambda e, cb=cb: e.tensor_copy(Zs[0][cb:cb + 64, :],
                                                            pB[cb:cb + 64, :])),
                  r=RB, w=("Zs0",))
                yield
                zi = 0
                for m in (1, 2, 4, 8, 16, 32):
                    mt = XpB[m]
                    for h in range(H):
                        O("tensor", (lambda e, mt=mt, h=h, cb=cb, zi=zi, m=m:
                                     e.matmul(pB[cb:cb + 64, 64 * h:64 * h + 64],
                                              mt[cb:cb + 64, 64 * h:64 * h + 64],
                                              Zs[zi][cb:cb + 64, 64 * h:64 * h + 64],
                                              start=False, stop=False, skip_group_check=True,
                                              tile_position=(cb, cb) if cb else None)),
                          r=(rs(f"Xp{m}"), f"Zs{zi}"), w=(RB[0],) if h < 8 else (RB[1],))
                    zi = 1 - zi
                    O("vector", (lambda e, cb=cb, zi=zi: e.tensor_copy(
                        Zs[zi][cb:cb + 64, :], pB[cb:cb + 64, :])),
                      r=RB, w=(f"Zs{zi}",))
                    yield
                O("vector", (lambda e, cb=cb, zi=zi, SMTbB=SMTbB: e.tensor_mul(
                    Wgt[cb:cb + 64, :].rearrange("p (h i) -> p h i", i=64),
                    Zs[zi][cb:cb + 64, :].rearrange("p (h i) -> p h i", i=64),
                    SMTbB[cb:cb + 64, 0:16].unsqueeze(2).broadcast_to([64, 16, 64]))),
                  r=(f"Zs{zi}", rs("SMTb")), w=("Wgt",))
                o1_ids = []
                for h in range(H):
                    hb = 64 * (h % 2)
                    pr = h // 2
                    col = 512 * c + 64 * pr
                    oid = O("tensor", (lambda e, hb=hb, pr=pr, col=col, m=hsl(qg, h, c), h=h:
                                 e.matmul(pC[hb:hb + 64, col:col + 64],
                                          STbf[hb:hb + 64, 64 * pr:64 * pr + 64], m,
                                          start=(h <= 1), stop=False, skip_group_check=True,
                                          tile_position=(hb, hb) if hb else None)),
                      r=("STbf", ("qg", h // 2)), w=(RC[c],))
                    o1_ids.append(oid)
                yield
                for h in range(H):
                    hb = 64 * (h % 2)
                    pr = h // 2
                    col = 512 * c + 64 * pr
                    O("tensor", (lambda e, hb=hb, cb=cb, col=col, zi=zi, h=h, P0sB=P0sB:
                                 e.matmul(pC[hb:hb + 64, col:col + 64],
                                          Zs[zi][cb:cb + 64, 64 * h:64 * h + 64],
                                          P0sB[cb:cb + 64, 64 * h:64 * h + 64],
                                          start=False, stop=(h >= H - 2), skip_group_check=True,
                                          tile_position=(cb, hb) if (cb or hb) else None)),
                      r=(f"Zs{zi}", rs("P0s")), w=(RC[c],),
                      after=tuple(o1_ids))
                yield
                for h in range(H):
                    hb = 64 * (h % 2)
                    pr = h // 2
                    O("tensor", (lambda e, hb=hb, pr=pr, cb=cb, h=h, KtoksB=KtoksB: e.matmul(
                        pS[hb:hb + 64, 64 * pr:64 * pr + 64],
                        KtoksB[cb:cb + 64, 128 * pr + hb:128 * pr + hb + 64],
                        Wgt[cb:cb + 64, 64 * h:64 * h + 64],
                        start=(h <= 1), stop=(h >= H - 2), skip_group_check=True,
                        tile_position=(cb, hb) if (cb or hb) else None)),
                      r=("Ktoks", "Wgt"), w=("pS",))
                O("vector", (lambda e, cg=cg: e.tensor_mul(
                    stsc[:, :].rearrange("p (r i) -> p r i", i=64),
                    ST[:, :].rearrange("p (r i) -> p r i", i=64),
                    gC[:, cg * 8:cg * 8 + 8].unsqueeze(2).broadcast_to([128, 8, 64]))),
                  r=("ST", "gC"), w=("stsc",))
                O("vector", lambda e: e.tensor_add(ST[:, :], stsc[:, :], pS[0:128, 0:512]),
                  r=("stsc", "pS"), w=("ST",))
                O("vector", lambda e: e.tensor_copy(STbf[:, :], ST[:, :]),
                  r=("ST",), w=("STbf",))
                ogap = gate[:, :].rearrange("p (pr tt) -> p pr tt", pr=8)[
                    :, :, t0 + 64 * c: t0 + 64 * c + 64]
                pcv = pC[:, 512 * c:512 * c + 512].rearrange("p (pr t) -> p pr t", pr=8)
                O("vector", (lambda e, ogap=ogap, pcv=pcv: e.scalar_tensor_tensor(
                    ogap, pcv, 1.0, ogap, ALU.mult, ALU.mult)),
                  r=(RC[c],) + tuple(("gate", b) for b in range(8)),
                  w=tuple(("gate", b) for b in range(8)))
                yield

        def drive(gens):
            gens = [g for g in gens if g is not None]
            while gens:
                for g in list(gens):
                    try:
                        next(g)
                    except StopIteration:
                        gens.remove(g)

        prev_seq = None
        if "delta" not in SKIP:
            for d in range(NDBL):
                sf = gen_SF(d, d % 2)
                drive([prev_seq, sf])
                prev_seq = gen_SEQ(d, d % 2)
            drive([prev_seq])

        if ti == 0:
            dump("d_og", gate, "gate")
            if KDUMP:
                O("gpsimd", (lambda e: e.dma_start(dbg["d_ST"][:, 0:512], ST[:, :])),
                  r=("ST",), w=("dbgdone",), dma=True, fam="dbgg")
        stream_mm("woS", DM, DM, pmap4(), lambda j: blk(gate, j), lambda j: ("gate", j),
                  epi_copy(vb, "vb"))
        if ti == 0:
            dump("d_ao", vb, "vb")
        ln_stats(vb, "vb")
        ln_apply(vb, "vb", xn, "xn", resid_t=xbf, resid_res="xbf")
        if ti == 0:
            dump("d_ybf", xn, "xn")

        ln_stats(xn, "xn")
        ln_apply(xn, "xn", qn, "qn")
        if ti == 0:
            dump("d_yn", qn, "qn")

        mov_yn = lambda j: blk(qn, j)
        res_yn = lambda j: ("qn", j)
        nob = DF // 128
        for pi, og0 in enumerate(range(0, nob, 2)):
            slots = []
            for wS in ("w1S", "w2S"):
                slot = wslot[wsl[0] % 3]
                sres = f"wslot{wsl[0] % 3}"
                wsl[0] += 1
                slots.append((slot, sres))
                O("sync", (lambda e, og0=og0, slot=slot, wS=wS:
                           e.dma_start(slot[:, 0:2048].rearrange("p (jj o) -> p jj o", jj=8),
                                       scr[wS][0:1024, 128 * og0:128 * og0 + 256]
                                       .rearrange("(jj p) o -> p jj o", p=128))),
                  r=tuple((wS, j) for j in range(8)), w=(sres,), dma=True, fam=sres)
            for wi, pmaps in ((0, PSETS[pi % 3]), (1, PSETS[(pi + 1) % 3])):
                slot, sres = slots[wi]
                for j in range(8):
                    for oi in range(2):
                        pap, bres = pmaps[oi]
                        stat = slot[:, j * 256 + oi * 128: j * 256 + (oi + 1) * 128]
                        O("tensor", (lambda e, pap=pap, stat=stat, mv=mov_yn(j),
                                     first=(j == 0), last=(j == 7):
                                     e.matmul(pap, stat, mv, start=first, stop=last)),
                          r=(sres, res_yn(j)), w=bres)
            for oi in range(2):
                o = og0 + oi
                papA, brA = PSETS[pi % 3][oi]
                papC, brC = PSETS[(pi + 1) % 3][oi]
                uap, ures = u1map[o]
                sq = (sqscr, sqscr2)[pi % 2]
                sqr = ("sqscr", "sqscr2")[pi % 2]
                O("scalar", (lambda e, o=o, papA=papA, sq=sq: e.activation(
                    sq[:, :], papA, AF.Silu, scale=S["s1s"][:, o:o + 1])),
                  r=brA + ("s1s",), w=(sqr,))
                O("vector", (lambda e, o=o, papC=papC, uap=uap, sq=sq: e.scalar_tensor_tensor(
                    uap, papC, S["s2s"][:, o:o + 1], sq[:, :], ALU.mult, ALU.mult)),
                  r=brC + ("s2s", sqr), w=(ures,))

        def epi_gelu(dmap, scal):
            def f(o, pap, bres):
                dap, dres = dmap[o]
                O("scalar", (lambda e, o=o, pap=pap, dap=dap: e.activation(
                    dap, pap, AF.Gelu, scale=S[scal][:, o:o + 1])),
                  r=bres + (scal,), w=(dres,))
            return f

        stream_mm("h1S", DF, DF, pmap4(), lambda j: u1map[j][0], lambda j: u1map[j][1],
                  epi_gelu(h2map, "hs1"))
        stream_mm("h2S", DF, DF, pmap4(), lambda j: h2map[j][0], lambda j: h2map[j][1],
                  epi_gelu(u1map, "hs2"))

        def epi_w3(o, pap, bres):
            O("scalar", (lambda e, o=o, pap=pap: e.mul(blk(qg, o), pap, S["w3s"][:, o:o + 1])),
              r=bres + ("w3s",), w=(("qg", o),))

        stream_mm("w3S", DF, DM, pmap4(), lambda j: u1map[j][0], lambda j: u1map[j][1],
                  epi_w3)

        ln_stats(qg, "qg")
        for b in range(8):
            ob = outb[b % 2]
            obres = f"outb{b % 2}"
            O("vector", (lambda e, b=b: e.tensor_sub(sqscr2[:, :], blk(qg, b), mbs[:, :])),
              r=(("qg", b), "mbs"), w=("sqscr2",))
            O("vector", (lambda e, b=b: e.tensor_mul(sqscr2[:, :], sqscr2[:, :], rbs[:, :])),
              r=("sqscr2", "rbs"), w=("sqscr2",))
            O("vector", (lambda e, b=b, ob=ob: e.tensor_add(ob[:, :], sqscr2[:, :],
                                                            blk(xn, b))),
              r=("sqscr2", ("xn", b)), w=(obres,))
            O("sync", (lambda e, b=b, ob=ob, ts=ts: e.dma_start(
                yt[128 * b:128 * b + 128, ts:ts + CT], ob[:, :])),
              r=(obres,), w=(("ydone", b % 2),), dma=True, fam=f"yo{b % 2}")

    O("sync", lambda e: e.nop(),
      r=(("ydone", 0), ("ydone", 1)) + (("dbgdone",) if KDUMP else ()))

    with nc.Block() as block:
        em.finalize(nc, block)
    return nc


_cache = {}


def prep_inputs(inputs, CT=512):
    cc = host_constants()
    base = {}
    for nm, src in [("wqT", "L0_attn_Wq"), ("wkT", "L0_attn_Wk"), ("wvT", "L0_attn_Wv"),
                    ("wgT", "L0_attn_Wg"), ("woT", "L0_attn_Wo"),
                    ("w1T", "L0_ffn_w1_w"), ("w2T", "L0_ffn_w2_w"),
                    ("h1T", "L0_ffn_h1_w"), ("h2T", "L0_ffn_h2_w"),
                    ("w3T", "L0_ffn_w3_w")]:
        base[nm] = np.ascontiguousarray(np.asarray(inputs[src], np.float32).T)
    base["wab"] = np.ascontiguousarray(np.concatenate(
        [np.asarray(inputs["L0_attn_Wa"], np.float32).T,
         np.asarray(inputs["L0_attn_Wb"], np.float32).T], axis=1))
    for nm, src in [("s1s", "L0_ffn_w1_s"), ("s2s", "L0_ffn_w2_s"),
                    ("hs1", "L0_ffn_h1_s"), ("hs2", "L0_ffn_h2_s")]:
        base[nm] = np.ascontiguousarray(np.asarray(inputs[src], np.float32).reshape(32, 128).T)
    base["w3s"] = np.ascontiguousarray(np.asarray(inputs["L0_ffn_w3_s"], np.float32).reshape(8, 128).T)
    base["bacol"] = np.ascontiguousarray(np.broadcast_to(
        np.asarray(inputs["L0_attn_ba"], np.float32).reshape(16, 1), (16, CT)))
    base["bbcol"] = np.ascontiguousarray(np.broadcast_to(
        np.asarray(inputs["L0_attn_bb"], np.float32).reshape(16, 1), (16, CT)))
    for nm in BF_CONSTS + F32_CONSTS:
        base[nm] = cc[nm]
    return base


def kernel(**inputs):
    T, CT = 8192, 512
    if "prog" not in _cache:
        _cache["prog"] = build_program(T, CT)
    nc = _cache["prog"]
    base = prep_inputs(inputs)
    x = np.asarray(inputs["L0_attn_x"], np.float32)
    in_maps = []
    for c in range(NCORES):
        m = dict(base)
        m["xt"] = np.ascontiguousarray(x[c].T)
        in_maps.append(m)
    res = run_bass_kernel_spmd(nc, in_maps, core_ids=list(range(NCORES)))
    out = np.stack([np.asarray(r["yt"]).T for r in res.results], axis=0)
    return out.astype(np.float32)


# version 24
# speedup vs baseline: 1.0717x; 1.0717x over previous
import os
import numpy as np
import concourse.bass as bass
import concourse.mybir as mybir
from concourse.bass_utils import run_bass_kernel_spmd

F32 = mybir.dt.float32
BF16 = mybir.dt.bfloat16
AF = mybir.ActivationFunctionType
ALU = mybir.AluOpType

H = 16
DH = 64
DM = 1024
DF = 4096
NCORES = 8
EPS_LN = 1e-5


class Em:

    ENGINES = ("tensor", "vector", "scalar", "sync", "gpsimd")

    TOTAL_FAMS = ("c_gp", "c_sync")

    def __init__(self):
        self.ops = []
        self.res_w = {}
        self.res_r = {}
        self.last_pe = None
        self.forced = {}

    def op(self, eng, fn, r=(), w=(), dma=False, fam=None, pe_sync=False,
           after=()):
        idx = len(self.ops)
        assert (fam is not None) == dma
        deps = set()
        if pe_sync and self.last_pe is not None:
            deps.add(self.last_pe)
            self.forced.setdefault(idx, set()).add(self.last_pe)
        for a in after:
            deps.add(a)
            self.forced.setdefault(idx, set()).add(a)
        for res in r:
            lw = self.res_w.get(res)
            if lw is not None:
                deps.add(lw)
        for res in w:
            lw = self.res_w.get(res)
            if lw is not None:
                deps.add(lw)
            deps.update(self.res_r.get(res, ()))
        for res in r:
            self.res_r.setdefault(res, []).append(idx)
        for res in w:
            self.res_w[res] = idx
            self.res_r[res] = []
        if eng == "tensor" and fam is None:
            self.last_pe = idx
        self.ops.append((eng, fn, deps, fam))
        return idx

    DUR = {"tensor": 160, "vector": 450, "scalar": 550, "sync": 1500, "gpsimd": 1500}
    LAT = 1100

    def schedule(self):
        import heapq
        ops = self.ops
        n = len(ops)
        succ = [[] for _ in range(n)]
        ndep = [0] * n
        for i, (eng, fn, deps, fam) in enumerate(ops):
            ndep[i] = len(deps)
            for d in deps:
                succ[d].append(i)
        prio = [0] * n
        for i in range(n - 1, -1, -1):
            eng = ops[i][0]
            d = self.DUR[eng]
            best = 0
            for sx in succ[i]:
                v = prio[sx] + (0 if ops[sx][0] == eng else self.LAT)
                if v > best:
                    best = v
            prio[i] = best + d
        for i, (eng, fn, deps, fam) in enumerate(ops):
            if fam in self.TOTAL_FAMS:
                prio[i] += 10 ** 12
        etime = {e: 0 for e in self.ENGINES}
        fin = [0] * n
        ready = []
        cnt = 0
        for i in range(n):
            if ndep[i] == 0:
                heapq.heappush(ready, (-prio[i], i))
        order = {e: [] for e in self.ENGINES}
        scheduled = 0
        def rtime(i, eng):
            rt = 0
            for d in ops[i][2]:
                t = fin[d] + (0 if ops[d][0] == eng else self.LAT)
                if t > rt:
                    rt = t
            return rt

        while ready:
            cand = [heapq.heappop(ready)]
            for _ in range(7):
                if not ready:
                    break
                cand.append(heapq.heappop(ready))
            best_k, best_st = 0, None
            for k, (negp, j) in enumerate(cand):
                e2 = ops[j][0]
                st = max(etime[e2], rtime(j, e2))
                if best_st is None or st < best_st - 100:
                    best_k, best_st = k, st
            negp, i = cand.pop(best_k)
            for c in cand:
                heapq.heappush(ready, c)
            eng, fn, deps, fam = ops[i]
            start = max(etime[eng], rtime(i, eng))
            fin[i] = start + self.DUR[eng]
            etime[eng] = fin[i]
            order[eng].append(i)
            scheduled += 1
            for sx in succ[i]:
                ndep[sx] -= 1
                if ndep[sx] == 0:
                    heapq.heappush(ready, (-prio[sx], sx))
        assert scheduled == len(ops)
        return order

    def finalize(self, nc, block):
        ops = self.ops
        order = self.schedule()
        pos = {}
        for e in self.ENGINES:
            for k, i in enumerate(order[e]):
                pos[i] = k
        fam_ord = {}
        fam_cnt = {}
        for e in self.ENGINES:
            for i in order[e]:
                fam = ops[i][3]
                if fam is not None:
                    fam_ord[i] = fam_cnt.get(fam, 0)
                    fam_cnt[fam] = fam_ord[i] + 1
        red = {}
        for i, (eng, fn, deps, fam) in enumerate(ops):
            best = {}
            for d in deps:
                deng, _, _, dfam = ops[d]
                key = ("d", dfam) if dfam is not None else ("e", deng)
                if deng == eng == "tensor" and dfam is None:
                    if d in self.forced.get(i, ()):
                        key = ("f", "tensor")
                    else:
                        continue
                b = best.get(key)
                if b is None or pos[d] > pos[b]:
                    best[key] = d
            red[i] = list(best.values())
        marked = set()
        for i, (eng, fn, deps, fam) in enumerate(ops):
            for d in red[i]:
                deng, _, _, dfam = ops[d]
                if dfam is None:
                    if not (deng == eng == "tensor") or d in self.forced.get(i, ()):
                        marked.add(d)
        mark_ord = {}
        cnt = {e: 0 for e in self.ENGINES}
        per_eng = order
        for e in self.ENGINES:
            for i in per_eng[e]:
                if i in marked:
                    cnt[e] += 1
                    mark_ord[i] = cnt[e]
        self._sem_ctx = []
        sems = {}
        fsems = {}
        for e in self.ENGINES:
            cm = nc.semaphore(f"es_{e}")
            self._sem_ctx.append(cm)
            sems[e] = cm.__enter__()
        for fam in fam_cnt:
            cm = nc.semaphore(f"df_{fam}")
            self._sem_ctx.append(cm)
            fsems[fam] = cm.__enter__()

        def body_for(ename):
            def body(engine):
                wm = {}
                for i in per_eng[ename]:
                    eng, fn, deps, fam = ops[i]
                    need = {}
                    for d in red[i]:
                        deng, _, _, dfam = ops[d]
                        if dfam is not None:
                            key = ("d", dfam)
                            if dfam in self.TOTAL_FAMS or str(dfam).startswith("dbg"):
                                val = 16 * fam_cnt[dfam]
                            else:
                                val = 16 * (fam_ord[d] + 1)
                        else:
                            if deng == ename == "tensor" and d not in self.forced.get(i, ()):
                                continue
                            key = ("e", deng)
                            val = mark_ord[d]
                        if need.get(key, -1) < val:
                            need[key] = val
                    for key in sorted(need, key=str):
                        val = need[key]
                        if wm.get(key, -1) >= val:
                            continue
                        wm[key] = val
                        engine.wait_ge((fsems[key[1]] if key[0] == "d" else sems[key[1]]), val)
                    inst = fn(engine)
                    if fam is not None:
                        inst.then_inc(fsems[fam], 16)
                    elif i in marked:
                        inst.then_inc(sems[ename], 1)
            return body

        for ename in self.ENGINES:
            getattr(block, ename)(body_for(ename))


def host_constants():
    c = {}
    ar = np.arange(64)
    a128 = np.arange(128) % 64
    c["c_id32"] = np.eye(128, dtype=np.float32)
    c["c_idbf"] = np.eye(128, dtype=np.float32)
    c["c_maskun"] = (ar[None, :] < a128[:, None]) * np.float32(1)
    c["c_maskpr"] = (ar[None, :] > a128[:, None]) * np.float32(1)
    c["c_maskpri"] = (ar[None, :] >= a128[:, None]) * np.float32(1)
    ind = np.zeros((16, 16, 64), np.float32)
    for j in range(16):
        ind[j, j, :] = 1.0
    c["c_ind64"] = np.concatenate([ind.reshape(16, 1024),
                                   np.zeros((112, 1024), np.float32)], 0)
    c["c_nonel"] = np.concatenate([np.tile(np.where(np.arange(128) < 64, -1.0, 0.0)
                                           .astype(np.float32), (16, 1)),
                                   np.zeros((112, 128), np.float32)], 0)
    c["c_noneh"] = np.concatenate([np.tile(np.where(np.arange(128) >= 64, -1.0, 0.0)
                                           .astype(np.float32), (16, 1)),
                                   np.zeros((112, 128), np.float32)], 0)
    b16 = np.zeros((16, 8, 128), np.float32)
    for blk in range(8):
        b16[2 * blk, blk, 0:64] = 1.0
        b16[2 * blk + 1, blk, 64:128] = 1.0
    c["c_b16"] = np.concatenate([b16.reshape(16, 1024),
                                 np.zeros((112, 1024), np.float32)], 0)
    sq = np.zeros((128, 8, 16), np.float32)
    for blk in range(8):
        sq[0:64, blk, 2 * blk] = 1.0
        sq[64:128, blk, 2 * blk + 1] = 1.0
    c["c_sq16"] = sq.reshape(128, 128)
    par = np.zeros((16, 128), np.float32)
    for m in range(16):
        par[m, 0:64] = 1.0 if m % 2 == 0 else 0.0
        par[m, 64:128] = 0.0 if m % 2 == 0 else 1.0
    c["c_parity"] = np.concatenate([par, np.zeros((112, 128), np.float32)], 0)
    pr = np.zeros((16, 8, 8), np.float32)
    for m in range(16):
        pr[m, :, m // 2] = 1.0
    c["c_pairrep"] = np.concatenate([pr.reshape(16, 64),
                                     np.zeros((112, 64), np.float32)], 0)
    c["c_onesrow"] = np.concatenate([np.ones((1, 128), np.float32),
                                     np.zeros((127, 128), np.float32)], 0)
    c["c_onescol"] = np.ones((128, 1), np.float32)
    return c

BF_CONSTS = ["c_idbf", "c_maskun", "c_maskpr", "c_maskpri", "c_ind64",
             "c_nonel", "c_noneh", "c_b16", "c_sq16", "c_parity",
             "c_pairrep", "c_onescol"]
F32_CONSTS = ["c_id32", "c_onesrow"]

WEIGHTS = [("wqT", DM, DM), ("wkT", DM, DM), ("wvT", DM, DM), ("wgT", DM, DM),
           ("woT", DM, DM), ("w1T", DM, DF), ("w2T", DM, DF),
           ("h1T", DF, DF), ("h2T", DF, DF), ("w3T", DF, DM)]


SKIP = set(os.environ.get("KSKIP", "").split(",")) - {""}
KDUMP = os.environ.get("KDUMP", "") == "1"


def build_program(T=8192, CT=512):
    nc = bass.Bass("TRN2", target_bir_lowering=False)
    NTILE = T // CT
    NDBL = CT // 128
    cc = host_constants()

    xt = nc.dram_tensor("xt", [DM, T], F32, kind="ExternalInput").ap()
    yt = nc.dram_tensor("yt", [DM, T], F32, kind="ExternalOutput").ap()
    w_in = {nm: nc.dram_tensor(nm, [a, b], F32, kind="ExternalInput").ap()
            for nm, a, b in WEIGHTS}
    w_in["wab"] = nc.dram_tensor("wab", [DM, 32], F32, kind="ExternalInput").ap()
    sc_in = {nm: nc.dram_tensor(nm, list(shp), F32, kind="ExternalInput").ap()
             for nm, shp in [("s1s", (128, 32)), ("s2s", (128, 32)),
                             ("hs1", (128, 32)), ("hs2", (128, 32)),
                             ("w3s", (128, 8)), ("bacol", (16, CT)), ("bbcol", (16, CT))]}
    cst_in = {nm: nc.dram_tensor(nm, list(cc[nm].shape), F32, kind="ExternalInput").ap()
              for nm in BF_CONSTS + F32_CONSTS}
    scr = {nm.replace("T", "S"): nc.dram_tensor(nm.replace("T", "S"), [a, b], BF16).ap()
           for nm, a, b in WEIGHTS}
    dbg = {}
    if KDUMP:
        for nm in ("d_xn", "d_qn", "d_kn", "d_vb", "d_gate", "d_og", "d_ao", "d_ybf",
                   "d_yn", "d_ST"):
            dbg[nm] = nc.dram_tensor(nm, [128, 8 * CT], BF16, kind="ExternalOutput").ap()

    def dump(nm, t8, res8):
        if not KDUMP:
            return
        O("sync", (lambda e: e.dma_start(dbg[nm][:, :], t8[:, 0:8 * CT])),
          r=tuple((res8, b) for b in range(8)) if res8 else (), w=("dbgdone",),
          dma=True, fam="dbg" + nm)

    em = Em()
    _ctx = []

    def sb(name, shape, dt):
        t = nc.sbuf_tensor(name, shape, dt)
        _ctx.append(t)
        return t.__enter__()

    def psum(name, shape, dt):
        t = nc.psum_tensor(name, shape, dt)
        _ctx.append(t)
        return t.__enter__()

    K = {}
    for nm in BF_CONSTS:
        K[nm] = sb("sb_" + nm, list(cc[nm].shape), BF16)
    for nm in F32_CONSTS:
        K[nm] = sb("sb_" + nm, list(cc[nm].shape), F32)
    S = {nm: sb("sb_" + nm, list(ap.shape), F32) for nm, ap in sc_in.items()}
    wabsb = sb("wabsb", [128, 256], BF16)
    wslot = [sb(f"wslot{i}", [128, 2048], BF16) for i in range(3)]
    ST = sb("ST", [128, 512], F32)
    STbf = sb("STbf", [128, 512], BF16)
    stsc = sb("stsc", [128, 512], F32)
    outb = [sb(f"outb{i}", [128, CT], F32) for i in range(2)]

    xbf = sb("xbf", [128, 8 * CT], BF16)
    xn = sb("xn", [128, 8 * CT], BF16)
    qn = sb("qn", [128, 8 * CT], BF16)
    qg = sb("qg", [128, 8 * CT], BF16)
    kn = sb("kn", [128, 8 * CT], BF16)
    kb = sb("kb", [128, 8 * CT], BF16)
    kgb = sb("kgb", [128, 8 * CT], BF16)
    vb = sb("vb", [128, 8 * CT], BF16)
    gate = sb("gate", [128, 8 * CT], BF16)

    sqscr = sb("sqscr", [128, CT], BF16)
    sqscr2 = sb("sqscr2", [128, CT], BF16)
    mbs = sb("mbs", [128, CT], BF16)
    rbs = sb("rbs", [128, CT], BF16)
    rowsA = sb("rowsA", [128, 2 * CT], F32)
    rowsB = sb("rowsB", [128, 2 * CT], F32)
    rowf1 = sb("rowf1", [128, 5 * CT], F32)
    rowf2 = sb("rowf2", [16, 5 * CT], F32)
    rG = (rowf1[0:16, 0:CT], ("rf1", 0))
    rgv = (rowf1[0:16, CT:2 * CT], ("rf1", 1))
    rgr = (rowf1[0:16, 2 * CT:3 * CT], ("rf1", 2))
    rbr = (rowf1[0:16, 3 * CT:4 * CT], ("rf1", 3))
    rla = (rowf1[0:16, 4 * CT:5 * CT], ("rf1", 4))
    rgp = rla
    r2 = [(rowf2[:, i * CT:(i + 1) * CT], ("rf2", i)) for i in range(5)]
    rowbA = sb("rowbA", [128, 4 * CT], BF16)
    rowbB = sb("rowbB", [128, 2 * CT], BF16)
    rowg = sb("rowg", [128, 2 * CT], BF16)
    mov2 = [sb("mov2_0", [128, 1024], BF16)]
    gC = sb("gC", [128, 64], F32)
    SMTb = sb("SMTb", [128, 16], BF16)
    epsb = sb("epsb", [1, 1], F32)

    escr = sb("escr", [128, 1024], F32)
    Eun = sb("Eun", [128, 1024], BF16)
    Eprs = sb("Eprs", [128, 1024], BF16)
    Epri = sb("Epri", [128, 1024], BF16)
    Aun_s = sb("Aun_s", [128, 1024], BF16)
    Aprs = sb("Aprs", [128, 1024], BF16)
    Akqs = sb("Akqs", [128, 1024], BF16)
    P0s = sb("P0s", [128, 1024], BF16)
    Xp = [{m: sb(f"Xp{m}_{i}", [128, 1024], BF16) for m in (1, 2, 4, 8, 16, 32)}
          for i in range(2)]
    Xua = sb("Xua", [128, 1024], BF16)
    Xub = sb("Xub", [128, 1024], BF16)
    Zs = [sb(f"Zs{i}", [128, 1024], BF16) for i in range(2)]
    Wgt = sb("Wgt", [128, 1024], BF16)
    VbTs = sb("VbTs", [128, 1024], BF16)
    Ktoks = sb("Ktoks", [128, 1024], BF16)
    P0s2 = [sb(f"P0s{i}", [128, 1024], BF16) for i in range(2)]
    SMTb2 = [sb(f"SMTb{i}", [128, 16], BF16) for i in range(2)]

    pA = psum("pA", [128, 1024], F32)
    pB = psum("pB", [128, 1024], F32)
    pC = psum("pC", [128, 1024], F32)
    pS = psum("pS", [128, 512], F32)
    ptr = psum("ptr", [128, 1024], BF16)
    RA = ("pA0", "pA1")
    RB = ("pB0", "pB1")
    RC = ("pC0", "pC1")
    PSETS = [
        [(pA[:, 0:CT], ("pA0",)), (pA[:, 512:512 + CT], ("pA1",))],
        [(pB[:, 0:CT], ("pB0",)), (pB[:, 512:512 + CT], ("pB1",))],
        [(pC[:, 0:CT], ("pC0",)), (pC[:, 512:512 + CT], ("pC1",))],
    ]

    O = em.op

    def blk(t8, b):
        return t8[:, b * CT:(b + 1) * CT]

    def mk_map(pairs):
        out = []
        for t8, rname in pairs:
            nb = t8.shape[1] // CT
            for b in range(nb):
                res = (rname, b) if rname in ("kn", "kb", "kgb", "vb", "gate", "xbf") else rname
                out.append((t8[:, b * CT:(b + 1) * CT], res))
        return out

    u1map = mk_map([(kn, "kn"), (kb, "kb"), (kgb, "kgb"), (vb, "vb")])
    h2map = mk_map([(gate, "gate"), (xbf, "xbf"),
                    (Xp[0][1], "Xp1_0"), (Xp[0][2], "Xp2_0"),
                    (Xp[0][4], "Xp4_0"), (Xp[0][8], "Xp8_0"),
                    (Xp[1][1], "Xp1_1"), (Xp[1][2], "Xp2_1"),
                    (Xp[1][4], "Xp4_1"), (Xp[1][8], "Xp8_1"),
                    (Zs[0], "Zs0"), (Zs[1], "Zs1")])[:32]

    for nm in BF_CONSTS:
        O("gpsimd", (lambda e, nm=nm: e.dma_start(K[nm][:, :], cst_in[nm][:, :])),
          w=(nm,), dma=True, fam="c_gp")
    for nm in F32_CONSTS:
        O("sync", (lambda e, nm=nm: e.dma_start(K[nm][:, :], cst_in[nm][:, :])),
          w=(nm,), dma=True, fam="c_sync")
    for nm in S:
        O("sync", (lambda e, nm=nm: e.dma_start(S[nm][:, :], sc_in[nm][:, :])),
          w=(nm,), dma=True, fam="c_sync")
    O("gpsimd", lambda e: e.dma_start(
        wabsb[:, :].rearrange("p (j c) -> p j c", j=8),
        w_in["wab"][:, :].rearrange("(j p) c -> p j c", p=128)),
      w=("wabsb",), dma=True, fam="c_gp")
    O("vector", lambda e: e.memset(ST[:, :], 0.0), w=("ST",))
    _pads = [(rowf1, [("rf1", i) for i in range(5)]),
             (rowbA, [(("rf2", 0), "bf"), (("rf2", 2), "bf"),
                      (("rf2", 1), "bf"), (("rf2", 3), "bf")]),
             (rowbB, [(("rf2", 4), "bf"), (("rf1", 3), "bf")]),
             (rowg, ["Ghib", "Glob"]), (mov2[0], [("m2", 0)]),
             (rowsA, ["rA0", "rA1"]), (rowsB, ["rB0", "rB1"])]
    for t_, rl in _pads:
        O("vector", (lambda e, t_=t_: e.memset(t_[:, :], 0.0)), w=tuple(rl))
    O("vector", lambda e: e.memset(STbf[:, :], 0.0), w=("STbf",))
    O("vector", lambda e: e.memset(epsb[:, :], EPS_LN), w=("epsb",))

    SLOTW = 8 * CT
    ppbufs = [(gate, tuple(("gate", b) for b in range(8))),
              (xbf, tuple(("xbf", b) for b in range(8)))]
    ppi = 0
    for nm, din, dout in WEIGHTS:
        dst = nm.replace("T", "S")
        for rr in range(din // 128):
            for c0 in range(0, dout, SLOTW):
                cw = min(SLOTW, dout - c0)
                buf, bres = ppbufs[ppi % 2]
                ppi += 1
                pfam = "g" if ppi % 2 == 1 else "x"
                O("gpsimd", (lambda e, nm=nm, rr=rr, c0=c0, cw=cw, buf=buf:
                             e.dma_start(buf[:, 0:cw],
                                         w_in[nm][128 * rr:128 * rr + 128, c0:c0 + cw])),
                  w=bres, dma=True, fam="ppg" + pfam)
                O("sync", (lambda e, dst=dst, rr=rr, c0=c0, cw=cw, buf=buf:
                           e.dma_start(scr[dst][128 * rr:128 * rr + 128, c0:c0 + cw],
                                       buf[:, 0:cw])),
                  r=bres, w=((dst, rr),), dma=True, fam="ppo" + pfam)

    def ln_stats(src_t, src_res):
        for b in range(8):
            sq = (sqscr, sqscr2)[b % 2]
            sqr = ("sqscr", "sqscr2")[b % 2]
            O("gpsimd", (lambda e, b=b, sq=sq: e.tensor_mul(sq[:, :], blk(src_t, b), blk(src_t, b))),
              r=((src_res, b),), w=(sqr,))
            O("tensor", (lambda e, b=b: e.matmul(pS[0:1, 0:CT], K["c_onescol"][:, :],
                                                 blk(src_t, b), start=(b == 0), stop=(b == 7))),
              r=((src_res, b), "c_onescol"), w=("pS",))
            O("tensor", (lambda e, b=b, sq=sq: e.matmul(pC[0:1, 0:CT], K["c_onescol"][:, :],
                                                 sq[:, :], start=(b == 0), stop=(b == 7))),
              r=(sqr, "c_onescol"), w=RC)
        O("vector", lambda e: e.tensor_single_scalar(rowsA[0:1, 0:CT], pS[0:1, 0:CT],
                                                     1.0 / DM, ALU.mult),
          r=("pS",), w=("rA0",))
        O("vector", lambda e: e.tensor_mul(rowsB[0:1, 0:CT], rowsA[0:1, 0:CT], rowsA[0:1, 0:CT]),
          r=("rA0",), w=("rB0",))
        O("vector", lambda e: e.scalar_tensor_tensor(rowsB[0:1, CT:2 * CT], pC[0:1, 0:CT],
                                                     1.0 / DM, rowsB[0:1, 0:CT],
                                                     ALU.mult, ALU.subtract),
          r=RC + ("rB0",), w=("rB1",))
        O("scalar", lambda e: e.activation(rowsA[0:1, CT:2 * CT], rowsB[0:1, CT:2 * CT],
                                           AF.Ln, bias=epsb[:, :]),
          r=("rB1", "epsb"), w=("rA1",))
        O("scalar", lambda e: e.activation(rowsB[0:1, 0:CT], rowsA[0:1, CT:2 * CT],
                                           AF.Exp, scale=-0.5),
          r=("rA1",), w=("rB0",))
        O("tensor", lambda e: e.matmul(pS[0:128, 0:CT], K["c_onesrow"][:, :],
                                       rowsA[0:128, 0:CT], start=True, stop=True),
          r=("rA0", "c_onesrow"), w=("pS",))
        O("tensor", lambda e: e.matmul(pC[0:128, 0:CT], K["c_onesrow"][:, :],
                                       rowsB[0:128, 0:CT], start=True, stop=True),
          r=("rB0", "c_onesrow"), w=RC)
        O("vector", lambda e: e.tensor_copy(mbs[:, :], pS[0:128, 0:CT]), r=("pS",), w=("mbs",))
        O("vector", lambda e: e.tensor_copy(rbs[:, :], pC[0:128, 0:CT]), r=RC, w=("rbs",))

    def ln_apply(src_t, src_res, dst_t, dst_res, resid_t=None, resid_res=None):
        for b in range(8):
            scr2 = sqscr2
            O("vector", (lambda e, b=b: e.tensor_sub(scr2[:, :], blk(src_t, b), mbs[:, :])),
              r=((src_res, b), "mbs"), w=("sqscr2",))
            if resid_t is None:
                O("vector", (lambda e, b=b: e.tensor_mul(blk(dst_t, b), scr2[:, :], rbs[:, :])),
                  r=("sqscr2", "rbs"), w=((dst_res, b),))
            else:
                O("vector", (lambda e, b=b: e.tensor_mul(scr2[:, :], scr2[:, :], rbs[:, :])),
                  r=("sqscr2", "rbs"), w=("sqscr2",))
                O("vector", (lambda e, b=b: e.tensor_add(blk(dst_t, b), scr2[:, :],
                                                         blk(resid_t, b))),
                  r=("sqscr2", (resid_res, b)), w=((dst_res, b),))

    wsl = [0]

    def stream_mm(wS, din, dout, pmaps4, mov_ap, mov_res, epi):
        nob = dout // 128
        nsb = din // 512
        for pi, og0 in enumerate(range(0, nob, 4)):
            pmaps = pmaps4
            width = 512
            for sbk in range(nsb):
                slot = wslot[wsl[0] % 3]
                sres = f"wslot{wsl[0] % 3}"
                wsl[0] += 1
                O("sync", (lambda e, sbk=sbk, og0=og0, slot=slot, width=width, wS=wS:
                           e.dma_start(slot[:, 0:4 * width].rearrange("p (jj o) -> p jj o", jj=4),
                                       scr[wS][512 * sbk:512 * sbk + 512,
                                               128 * og0:128 * og0 + width]
                                       .rearrange("(jj p) o -> p jj o", p=128))),
                  r=tuple((wS, 4 * sbk + j) for j in range(4)), w=(sres,), dma=True, fam=sres)
                for jj in range(4):
                    j = 4 * sbk + jj
                    for oi in range(4):
                        pap, bres = pmaps[oi]
                        stat = slot[:, jj * width + oi * 128: jj * width + (oi + 1) * 128]
                        O("tensor", (lambda e, pap=pap, stat=stat, mv=mov_ap(j),
                                     first=(sbk == 0 and jj == 0),
                                     last=(sbk == nsb - 1 and jj == 3):
                                     e.matmul(pap, stat, mv, start=first, stop=last)),
                          r=(sres, mov_res(j)), w=bres)
            for oi in range(4):
                epi(og0 + oi, *pmaps[oi])

    def pmap4():
        return [(pA[:, 0:CT], ("pA0",)), (pA[:, 512:512 + CT], ("pA1",)),
                (pB[:, 0:CT], ("pB0",)), (pB[:, 512:512 + CT], ("pB1",))]

    def logsig(psrc, psrc_res, bias, dst, dst_res):
        (zap, zr), (sap, sr), (map_, mr) = r2[0], r2[1], r2[2]
        O("vector", (lambda e: e.tensor_copy(zap, psrc)),
          r=(psrc_res,), w=(zr,))
        O("vector", (lambda e: e.tensor_add(zap, zap, S[bias][:, :])),
          r=(zr, bias), w=(zr,))
        O("scalar", lambda e: e.activation(sap, zap, AF.Abs),
          r=(zr,), w=(sr,))
        O("scalar", lambda e: e.activation(sap, sap, AF.Exp, scale=-1.0),
          r=(sr,), w=(sr,))
        O("scalar", lambda e: e.activation(sap, sap, AF.Ln, bias=1.0),
          r=(sr,), w=(sr,))
        O("vector", lambda e: e.tensor_single_scalar(map_, zap, 0.0, ALU.min),
          r=(zr,), w=(mr,))
        O("vector", (lambda e: e.tensor_sub(dst, map_, sap)),
          r=(mr, sr), w=(dst_res,))

    seg = CT // 64

    for ti in range(NTILE):
        ts = ti * CT
        O("gpsimd", (lambda e, ts=ts: e.dma_start(
            xbf[:, :].rearrange("p (b t) -> p b t", b=8),
            xt[:, ts:ts + CT].rearrange("(b p) t -> p b t", p=128))),
          w=tuple(("xbf", b) for b in range(8)), dma=True, fam="xin")

        ln_stats(xbf, "xbf")
        ln_apply(xbf, "xbf", xn, "xn")
        if ti == 0:
            dump("d_xn", xn, "xn")

        def epi_copy(dst_t, dres):
            def f(o, pap, bres):
                O("scalar", (lambda e, o=o, pap=pap: e.copy(blk(dst_t, o), pap)),
                  r=bres, w=((dres, o),))
            return f

        def epi_silu(dst_t, dres):
            def f(o, pap, bres):
                O("scalar", (lambda e, o=o, pap=pap: e.activation(blk(dst_t, o), pap, AF.Silu)),
                  r=bres, w=((dres, o),))
            return f

        mov_xn = lambda j: blk(xn, j)
        res_xn = lambda j: ("xn", j)
        stream_mm("wqS", DM, DM, pmap4(), mov_xn, res_xn, epi_copy(qg, "qg"))
        stream_mm("wkS", DM, DM, pmap4(), mov_xn, res_xn, epi_copy(kn, "kn"))
        stream_mm("wvS", DM, DM, pmap4(), mov_xn, res_xn, epi_copy(vb, "vb"))
        stream_mm("wgS", DM, DM, pmap4(), mov_xn, res_xn, epi_silu(gate, "gate"))

        for j in range(8):
            O("tensor", (lambda e, j=j: e.matmul(pA[0:16, 0:CT], wabsb[:, j * 32:j * 32 + 16],
                                                 blk(xn, j), start=(j == 0), stop=(j == 7))),
              r=(("xn", j), "wabsb"), w=("pA0",))
            O("tensor", (lambda e, j=j: e.matmul(pB[0:16, 0:CT],
                                                 wabsb[:, j * 32 + 16:j * 32 + 32],
                                                 blk(xn, j), start=(j == 0), stop=(j == 7))),
              r=(("xn", j), "wabsb"), w=("pB0",))

        logsig(pA[0:16, 0:CT], "pA0", "bacol", rla[0], rla[1])
        O("vector", lambda e: e.tensor_single_scalar(rla[0], rla[0], -60.0, ALU.max),
          r=(rla[1],), w=(rla[1],))
        logsig(pB[0:16, 0:CT], "pB0", "bbcol", r2[3][0], r2[3][1])
        O("scalar", lambda e: e.activation(rbr[0], r2[3][0], AF.Exp),
          r=(r2[3][1],), w=(rbr[1],))
        O("vector", lambda e: e.tensor_copy(rG[0], rla[0]),
          r=(rla[1],), w=(rG[1],))
        for sh in (1, 2, 4, 8, 16, 32):
            O("vector", (lambda e, sh=sh: e.tensor_add(
                rgp[0].rearrange("p (s c) -> p s c", c=64)[:, :, sh:64],
                rG[0].rearrange("p (s c) -> p s c", c=64)[:, :, sh:64],
                rG[0].rearrange("p (s c) -> p s c", c=64)[:, :, 0:64 - sh])),
              r=(rG[1],), w=(rgp[1],))
            O("vector", (lambda e, sh=sh: e.tensor_copy(
                rgp[0].rearrange("p (s c) -> p s c", c=64)[:, :, 0:sh],
                rG[0].rearrange("p (s c) -> p s c", c=64)[:, :, 0:sh])),
              r=(rG[1],), w=(rgp[1],))
            O("vector", lambda e: e.tensor_copy(rG[0], rgp[0]),
              r=(rgp[1],), w=(rG[1],))
        O("scalar", lambda e: e.activation(rgv[0], rG[0], AF.Exp),
          r=(rG[1],), w=(rgv[1],))
        gend = rG[0].rearrange("p (s c) -> p s c", c=64)[:, :, 63:64] \
            .broadcast_to([16, seg, 64])
        O("vector", lambda e: e.tensor_sub(
            rgr[0].rearrange("p (s c) -> p s c", c=64),
            rG[0].rearrange("p (s c) -> p s c", c=64), gend),
          r=(rG[1],), w=(rgr[1],))
        O("scalar", lambda e: e.activation(rgr[0], rgr[0], AF.Exp, scale=-1.0),
          r=(rgr[1],), w=(rgr[1],))
        O("vector", lambda e: e.tensor_copy(rowg[0:16, 0:CT], rG[0]),
          r=(rG[1],), w=("Ghib",))
        O("vector", lambda e: e.tensor_sub(rowg[0:16, CT:2 * CT], rG[0],
                                           rowg[0:16, 0:CT]),
          r=(rG[1], "Ghib"), w=("Glob",))
        gvend = rgv[0].rearrange("p (s c) -> p s c", c=64)[:, :, 63:64] \
            .broadcast_to([16, seg, 8])
        O("vector", (lambda e: e.tensor_mul(
            mov2[0][0:16, 0:8 * seg].rearrange("p (s c) -> p s c", c=8), gvend,
            K["c_pairrep"][0:16, 0:8 * seg].rearrange("p (s c) -> p s c", c=8))),
          r=(rgv[1], "c_pairrep"), w=(("m2", 0),))
        O("tensor", lambda e: e.matmul(pS[0:128, 0:8 * seg], K["c_parity"][:, :],
                                       mov2[0][:, 0:8 * seg], start=True, stop=True),
          r=(("m2", 0), "c_parity"), w=("pS",))
        O("vector", lambda e: e.tensor_copy(gC[:, 0:8 * seg], pS[0:128, 0:8 * seg]),
          r=("pS",), w=("gC",))

        for raw_t, raw_res, pap, bres, dst, dres in (
                (qg, "qg", pS[0:16, 0:CT], ("pS",), r2[0][0], r2[0][1]),
                (kn, "kn", pC[0:16, 0:CT], RC, r2[1][0], r2[1][1])):
            for b in range(8):
                sq = (sqscr, sqscr2)[b % 2]
                sqr = ("sqscr", "sqscr2")[b % 2]
                O("gpsimd", (lambda e, b=b, raw_t=raw_t, sq=sq: e.tensor_mul(
                    sq[:, :], blk(raw_t, b), blk(raw_t, b))),
                  r=((raw_res, b),), w=(sqr,))
                O("tensor", (lambda e, b=b, pap=pap, sq=sq: e.matmul(
                    pap, K["c_sq16"][:, b * 16:b * 16 + 16], sq[:, :],
                    start=(b == 0), stop=(b == 7))),
                  r=(sqr, "c_sq16"), w=bres)
            O("scalar", (lambda e, pap=pap, dst=dst: e.activation(dst, pap, AF.Ln)),
              r=bres, w=(dres,))
            O("scalar", (lambda e, dst=dst: e.activation(dst, dst, AF.Exp, scale=-0.5)),
              r=(dres,), w=(dres,))
            O("vector", (lambda e, dst=dst: e.tensor_single_scalar(dst, dst, 1e6, ALU.min)),
              r=(dres,), w=(dres,))
        O("vector", lambda e: e.tensor_mul(r2[2][0], r2[0][0], rgv[0]),
          r=(r2[0][1], rgv[1]), w=(r2[2][1],))
        O("vector", lambda e: e.scalar_tensor_tensor(r2[3][0], r2[1][0],
                                                     -1.0, rbr[0],
                                                     ALU.mult, ALU.mult),
          r=(r2[1][1], rbr[1]), w=(r2[3][1],))
        O("vector", lambda e: e.tensor_mul(r2[4][0], r2[3][0], rgv[0]),
          r=(r2[3][1], rgv[1]), w=(r2[4][1],))
        rowsrc = [(r2[0][1], r2[0][0], rowbA[0:16, 0:CT]),
                  (r2[2][1], r2[2][0], rowbA[0:16, CT:2 * CT]),
                  (r2[1][1], r2[1][0], rowbA[0:16, 2 * CT:3 * CT]),
                  (r2[3][1], r2[3][0], rowbA[0:16, 3 * CT:4 * CT]),
                  (r2[4][1], r2[4][0], rowbB[0:16, 0:CT]),
                  (rbr[1], rbr[0], rowbB[0:16, CT:2 * CT])]
        for res, src, dst in rowsrc:
            O("vector", (lambda e, src=src, dst=dst: e.tensor_copy(dst, src)),
              r=(res,), w=((res, "bf"),))
        rowap = {}
        rowres = {}
        fullv = [rowbA[:, 0:CT], rowbA[:, CT:2 * CT], rowbA[:, 2 * CT:3 * CT],
                 rowbA[:, 3 * CT:4 * CT], rowbB[:, 0:CT], rowbB[:, CT:2 * CT]]
        for (nm, (res, _, dst)), fv in zip(
                zip(("rnq", "rg", "rnk", "rkb", "rkg", "brow"), rowsrc), fullv):
            rowap[nm] = fv
            rowres[nm] = (res, "bf")

        for b in range(8):
            def bc(res, pap, bres):
                O("tensor", (lambda e, res=res, pap=pap, b=b: e.matmul(
                    pap, K["c_b16"][:, b * 128:(b + 1) * 128], rowap[res],
                    start=True, stop=True)),
                  r=(rowres[res], "c_b16"), w=bres)
            bc("rnq", pS[0:128, 0:CT], ("pS",))
            O("vector", (lambda e, b=b: e.scalar_tensor_tensor(
                blk(qn, b), pS[0:128, 0:CT], 1.0, blk(qg, b), ALU.mult, ALU.mult)),
              r=("pS", ("qg", b)), w=(("qn", b),))
            bc("rg", pC[0:128, 0:CT], RC)
            O("vector", (lambda e, b=b: e.scalar_tensor_tensor(
                blk(qg, b), pC[0:128, 0:CT], 1.0, blk(qg, b), ALU.mult, ALU.mult)),
              r=RC + (("qg", b),), w=(("qg", b),))
            bc("rkb", pS[0:128, 0:CT], ("pS",))
            O("vector", (lambda e, b=b: e.scalar_tensor_tensor(
                blk(kb, b), pS[0:128, 0:CT], 1.0, blk(kn, b), ALU.mult, ALU.mult)),
              r=("pS", ("kn", b)), w=(("kb", b),))
            bc("rkg", pC[0:128, 0:CT], RC)
            O("vector", (lambda e, b=b: e.scalar_tensor_tensor(
                blk(kgb, b), pC[0:128, 0:CT], 1.0, blk(kn, b), ALU.mult, ALU.mult)),
              r=RC + (("kn", b),), w=(("kgb", b),))
            bc("rnk", pS[0:128, 0:CT], ("pS",))
            O("vector", (lambda e, b=b: e.scalar_tensor_tensor(
                blk(kn, b), pS[0:128, 0:CT], 1.0, blk(kn, b), ALU.mult, ALU.mult)),
              r=("pS", ("kn", b)), w=(("kn", b),))
            bc("brow", pC[0:128, 0:CT], RC)
            O("vector", (lambda e, b=b: e.scalar_tensor_tensor(
                blk(vb, b), pC[0:128, 0:CT], 1.0, blk(vb, b), ALU.mult, ALU.mult)),
              r=RC + (("vb", b),), w=(("vb", b),))

        if ti == 0:
            dump("d_qn", qn, "qn")
            dump("d_kn", kn, "kn")
            dump("d_vb", vb, "vb")
            dump("d_gate", gate, "gate")

        def gen_SF(d, bi):
            t0 = d * 128
            XpB, P0sB, SMTbB = Xp[bi], P0s2[bi], SMTb2[bi]
            rs = lambda nm: f"{nm}_{bi}"

            def hsl(t8, h, c):
                return blk(t8, h // 2)[64 * (h % 2):64 * (h % 2) + 64,
                                       t0 + 64 * c: t0 + 64 * c + 64]

            O("tensor", (lambda e, t0=t0: e.transpose(
                pS[0:128, 0:128], rowf1[:, 2 * CT + t0:2 * CT + t0 + 128],
                K["c_id32"][:, :])),
              r=(rgr[1], "c_id32"), w=("pS",))
            O("vector", (lambda e, SMTbB=SMTbB: e.tensor_copy(SMTbB[:, 0:16], pS[0:128, 0:16])),
              r=("pS",), w=(rs("SMTb"),))
            for g2 in range(2):
                for rres, rap in (("Ghib", rowg[:, t0:t0 + 128]),
                                  ("Glob", rowg[:, CT + t0:CT + t0 + 128])):
                    O("tensor", (lambda e, g2=g2, rap=rap, first=(rres == "Ghib"):
                                 e.matmul(pA[:, 512 * g2:512 * g2 + 512], rap,
                                          K["c_ind64"][:, 512 * g2:512 * g2 + 512],
                                          start=first, stop=False)),
                      r=(rres, "c_ind64"), w=(RA[g2],))
            yield
            for half, nmsk in ((0, "c_nonel"), (1, "c_noneh")):
                for hl, rres in ((0, "Ghib"), (1, "Glob")):
                    c0r = (0, CT)[hl]
                    src = rowg[0:16, c0r + t0 + 64 * half: c0r + t0 + 64 * half + 64] \
                        .unsqueeze(1).broadcast_to([16, 16, 64])
                    O("vector", (lambda e, src=src: e.tensor_mul(
                        mov2[0][0:16, :].rearrange("p (h c) -> p h c", c=64),
                        K["c_ind64"][0:16, :].rearrange("p (h c) -> p h c", c=64), src)),
                      r=(rres, "c_ind64"), w=(("m2", 0),))
                    for g2 in range(2):
                        last = (half == 1 and hl == 1)
                        O("tensor", (lambda e, g2=g2, nmsk=nmsk, last=last:
                                     e.matmul(pA[:, 512 * g2:512 * g2 + 512], K[nmsk][:, :],
                                              mov2[0][:, 512 * g2:512 * g2 + 512],
                                              start=False, stop=last)),
                          r=(("m2", 0), nmsk), w=(RA[g2],))
            yield
            O("vector", lambda e: e.tensor_single_scalar(escr[:, :], pA[:, :], 0.0, ALU.min),
              r=RA, w=("escr",))
            O("scalar", lambda e: e.activation(Eun[:, :], escr[:, :], AF.Exp),
              r=("escr",), w=("Eun",))
            O("vector", lambda e: e.tensor_scalar(escr[:, :], pA[:, :], -1.0, 0.0,
                                                  ALU.mult, ALU.min),
              r=RA, w=("escr",))
            O("scalar", lambda e: e.activation(Eprs[:, :], escr[:, :], AF.Exp),
              r=("escr",), w=("Eprs",))
            O("vector", lambda e: e.tensor_mul(
                Epri[:, :].rearrange("p (h c) -> p h c", c=64),
                Eprs[:, :].rearrange("p (h c) -> p h c", c=64),
                K["c_maskpri"][:, :].unsqueeze(1).broadcast_to([128, 16, 64])),
              r=("Eprs", "c_maskpri"), w=("Epri",))
            O("vector", lambda e: e.tensor_mul(
                Eprs[:, :].rearrange("p (h c) -> p h c", c=64),
                Eprs[:, :].rearrange("p (h c) -> p h c", c=64),
                K["c_maskpr"][:, :].unsqueeze(1).broadcast_to([128, 16, 64])),
              r=("Eprs", "c_maskpr"), w=("Eprs",))
            O("vector", lambda e: e.tensor_mul(
                Eun[:, :].rearrange("p (h c) -> p h c", c=64),
                Eun[:, :].rearrange("p (h c) -> p h c", c=64),
                K["c_maskun"][:, :].unsqueeze(1).broadcast_to([128, 16, 64])),
              r=("Eun", "c_maskun"), w=("Eun",))
            yield
            ablocks = [(0, 0), (0, 64), (64, 64), (64, 0)]
            for akind, (s1f, s2f, evt, evr) in enumerate((
                    (kb, kn, Aun_s, "Aun_s"), (kn, kb, Aprs, "Aprs"),
                    (kn, qn, Akqs, "Akqs"))):
                s1r = ("kb", "kn", "kn")[akind]
                s2r = ("kn", "kb", "qn")[akind]
                row0_ids = []
                for bi2, (hb, cb) in enumerate(ablocks):
                    c = cb // 64
                    for h in range(hb // 64, H, 2):
                        tp = (hb, cb) if (hb or cb) else None
                        oid = O("tensor", (lambda e, sA=hsl(s1f, h, c), m=hsl(s2f, h, c),
                                           h=h, cb=cb, tp=tp:
                                           e.matmul(pA[cb:cb + 64, 64 * h:64 * h + 64], sA, m,
                                                    start=True, stop=True, tile_position=tp)),
                              r=((s1r, h // 2), (s2r, h // 2)),
                              w=(RA[0],) if h < 8 else (RA[1],),
                              after=tuple(row0_ids) if bi2 >= 2 else ())
                        if bi2 < 2:
                            row0_ids.append(oid)
                ev_eng = ("vector", "scalar", "vector")[akind]
                fn = (lambda e, evt=evt: e.tensor_copy(evt[:, :], pA[:, :])) \
                    if ev_eng == "vector" else \
                    (lambda e, evt=evt: e.copy(evt[:, :], pA[:, :]))
                O(ev_eng, fn, r=RA, w=(evr,))
                yield
            O("vector", (lambda e, XpB=XpB: e.tensor_mul(XpB[1][:, :], Aprs[:, :], Eprs[:, :])),
              r=("Aprs", "Eprs"), w=(rs("Xp1"),))
            O("vector", lambda e: e.tensor_mul(Xua[:, :], Aun_s[:, :], Eun[:, :]),
              r=("Aun_s", "Eun"), w=("Xua",))
            O("vector", (lambda e, P0sB=P0sB: e.tensor_mul(P0sB[:, :], Akqs[:, :], Epri[:, :])),
              r=("Akqs", "Epri"), w=(rs("P0s"),))
            yield
            prevp, prevpt = rs("Xp1"), XpB[1]
            prevu, prevut = "Xua", Xua
            for m in (2, 4, 8, 16, 32):
                for h in range(H):
                    for c in range(2):
                        cb = 64 * c
                        tp = (cb, cb) if cb else None
                        O("tensor", (lambda e, h=h, cb=cb, tp=tp, su=prevut, sp=prevpt:
                                     e.matmul(pA[cb:cb + 64, 64 * h:64 * h + 64],
                                              su[cb:cb + 64, 64 * h:64 * h + 64],
                                              sp[cb:cb + 64, 64 * h:64 * h + 64],
                                              start=True, stop=True, tile_position=tp)),
                          r=(prevu, prevp), w=(RA[0],) if h < 8 else (RA[1],))
                O("vector", (lambda e, m=m, XpB=XpB: e.tensor_copy(XpB[m][:, :], pA[:, :])),
                  r=RA, w=(rs(f"Xp{m}"),))
                yield
                if m != 32:
                    for h in range(H):
                        for c in range(2):
                            cb = 64 * c
                            tp = (cb, cb) if cb else None
                            O("tensor", (lambda e, h=h, cb=cb, tp=tp, su=prevut, sp=prevpt:
                                         e.matmul(pA[cb:cb + 64, 64 * h:64 * h + 64],
                                                  sp[cb:cb + 64, 64 * h:64 * h + 64],
                                                  su[cb:cb + 64, 64 * h:64 * h + 64],
                                                  start=True, stop=True, tile_position=tp)),
                              r=(prevu, prevp), w=(RA[0],) if h < 8 else (RA[1],))
                    nxt = Xub if prevut is Xua else Xua
                    nxtr = "Xub" if prevut is Xua else "Xua"
                    O("scalar", (lambda e, nxt=nxt: e.copy(nxt[:, :], pA[:, :])),
                      r=RA, w=(nxtr,))
                    prevu, prevut = nxtr, nxt
                    yield
                prevp, prevpt = rs(f"Xp{m}"), XpB[m]

        def gen_SEQ(d, bi):
            t0 = d * 128
            XpB, P0sB, SMTbB = Xp[bi], P0s2[bi], SMTb2[bi]
            VbTsB, KtoksB = VbTs, Ktoks
            rs = lambda nm: f"{nm}_{bi}"

            def hsl(t8, h, c):
                return blk(t8, h // 2)[64 * (h % 2):64 * (h % 2) + 64,
                                       t0 + 64 * c: t0 + 64 * c + 64]

            for src_t, sres, dst, dres in ((vb, "vb", VbTsB, "VbTs"),
                                           (kn, "kn", KtoksB, "Ktoks")):
                for pr in range(8):
                    for c in range(2):
                        cb = 64 * c
                        src = blk(src_t, pr)[:, t0 + 64 * c: t0 + 64 * c + 64]
                        O("tensor", (lambda e, src=src, pr=pr, cb=cb: e.transpose(
                            ptr[cb:cb + 64, 128 * pr:128 * pr + 128], src,
                            K["c_idbf"][:, :],
                            tile_position=(0, cb) if cb else None)),
                          r=((sres, pr), "c_idbf"), w=("ptr",))
                O("vector", (lambda e, dst=dst: e.tensor_copy(dst[:, :], ptr[:, :])),
                  r=("ptr",), w=(dres,))
                yield
            for c in range(2):
                cb = 64 * c
                cg = 2 * d + c
                prev_ids = []
                for g2 in range(2):
                    oid = O("tensor", (lambda e, cb=cb, g2=g2, VbTsB=VbTsB: e.matmul(
                        pB[cb:cb + 64, 512 * g2:512 * g2 + 512],
                        K["c_idbf"][cb:cb + 64, cb:cb + 64],
                        VbTsB[cb:cb + 64, 512 * g2:512 * g2 + 512],
                        start=True, stop=False, skip_group_check=True,
                        tile_position=(cb, cb) if cb else None)),
                      r=("VbTs", "c_idbf"), w=(RB[g2],))
                    prev_ids.append(oid)
                for par in (0, 1):
                    cur_ids = []
                    for h in range(par, H, 2):
                        hb = 64 * par
                        pr = h // 2
                        oid = O("tensor", (lambda e, sA=hsl(kgb, h, c), hb=hb, pr=pr, cb=cb, h=h:
                                     e.matmul(pB[cb:cb + 64, 64 * h:64 * h + 64], sA,
                                              STbf[hb:hb + 64, 64 * pr:64 * pr + 64],
                                              start=False, stop=False, skip_group_check=True,
                                              tile_position=(hb, cb) if (hb or cb) else None)),
                          r=(("kgb", h // 2), "STbf"), w=(RB[0],) if h < 8 else (RB[1],),
                          after=tuple(prev_ids))
                        cur_ids.append(oid)
                    prev_ids = cur_ids
                O("vector", (lambda e, cb=cb: e.tensor_copy(Zs[0][cb:cb + 64, :],
                                                            pB[cb:cb + 64, :])),
                  r=RB, w=("Zs0",))
                yield
                zi = 0
                for m in (1, 2, 4, 8, 16, 32):
                    mt = XpB[m]
                    for h in range(H):
                        O("tensor", (lambda e, mt=mt, h=h, cb=cb, zi=zi, m=m:
                                     e.matmul(pB[cb:cb + 64, 64 * h:64 * h + 64],
                                              mt[cb:cb + 64, 64 * h:64 * h + 64],
                                              Zs[zi][cb:cb + 64, 64 * h:64 * h + 64],
                                              start=False, stop=False, skip_group_check=True,
                                              tile_position=(cb, cb) if cb else None)),
                          r=(rs(f"Xp{m}"), f"Zs{zi}"), w=(RB[0],) if h < 8 else (RB[1],))
                    zi = 1 - zi
                    O("vector", (lambda e, cb=cb, zi=zi: e.tensor_copy(
                        Zs[zi][cb:cb + 64, :], pB[cb:cb + 64, :])),
                      r=RB, w=(f"Zs{zi}",))
                    yield
                O("vector", (lambda e, cb=cb, zi=zi, SMTbB=SMTbB: e.tensor_mul(
                    Wgt[cb:cb + 64, :].rearrange("p (h i) -> p h i", i=64),
                    Zs[zi][cb:cb + 64, :].rearrange("p (h i) -> p h i", i=64),
                    SMTbB[cb:cb + 64, 0:16].unsqueeze(2).broadcast_to([64, 16, 64]))),
                  r=(f"Zs{zi}", rs("SMTb")), w=("Wgt",))
                o1_ids = []
                for h in range(H):
                    hb = 64 * (h % 2)
                    pr = h // 2
                    col = 512 * c + 64 * pr
                    oid = O("tensor", (lambda e, hb=hb, pr=pr, col=col, m=hsl(qg, h, c), h=h:
                                 e.matmul(pC[hb:hb + 64, col:col + 64],
                                          STbf[hb:hb + 64, 64 * pr:64 * pr + 64], m,
                                          start=(h <= 1), stop=False, skip_group_check=True,
                                          tile_position=(hb, hb) if hb else None)),
                      r=("STbf", ("qg", h // 2)), w=(RC[c],))
                    o1_ids.append(oid)
                yield
                for h in range(H):
                    hb = 64 * (h % 2)
                    pr = h // 2
                    col = 512 * c + 64 * pr
                    O("tensor", (lambda e, hb=hb, cb=cb, col=col, zi=zi, h=h, P0sB=P0sB:
                                 e.matmul(pC[hb:hb + 64, col:col + 64],
                                          Zs[zi][cb:cb + 64, 64 * h:64 * h + 64],
                                          P0sB[cb:cb + 64, 64 * h:64 * h + 64],
                                          start=False, stop=(h >= H - 2), skip_group_check=True,
                                          tile_position=(cb, hb) if (cb or hb) else None)),
                      r=(f"Zs{zi}", rs("P0s")), w=(RC[c],),
                      after=tuple(o1_ids))
                yield
                for h in range(H):
                    hb = 64 * (h % 2)
                    pr = h // 2
                    O("tensor", (lambda e, hb=hb, pr=pr, cb=cb, h=h, KtoksB=KtoksB: e.matmul(
                        pS[hb:hb + 64, 64 * pr:64 * pr + 64],
                        KtoksB[cb:cb + 64, 128 * pr + hb:128 * pr + hb + 64],
                        Wgt[cb:cb + 64, 64 * h:64 * h + 64],
                        start=(h <= 1), stop=(h >= H - 2), skip_group_check=True,
                        tile_position=(cb, hb) if (cb or hb) else None)),
                      r=("Ktoks", "Wgt"), w=("pS",))
                O("vector", (lambda e, cg=cg: e.tensor_mul(
                    stsc[:, :].rearrange("p (r i) -> p r i", i=64),
                    ST[:, :].rearrange("p (r i) -> p r i", i=64),
                    gC[:, cg * 8:cg * 8 + 8].unsqueeze(2).broadcast_to([128, 8, 64]))),
                  r=("ST", "gC"), w=("stsc",))
                O("vector", lambda e: e.tensor_add(ST[:, :], stsc[:, :], pS[0:128, 0:512]),
                  r=("stsc", "pS"), w=("ST",))
                O("vector", lambda e: e.tensor_copy(STbf[:, :], ST[:, :]),
                  r=("ST",), w=("STbf",))
                ogap = gate[:, :].rearrange("p (pr tt) -> p pr tt", pr=8)[
                    :, :, t0 + 64 * c: t0 + 64 * c + 64]
                pcv = pC[:, 512 * c:512 * c + 512].rearrange("p (pr t) -> p pr t", pr=8)
                O("vector", (lambda e, ogap=ogap, pcv=pcv: e.scalar_tensor_tensor(
                    ogap, pcv, 1.0, ogap, ALU.mult, ALU.mult)),
                  r=(RC[c],) + tuple(("gate", b) for b in range(8)),
                  w=tuple(("gate", b) for b in range(8)))
                yield

        def drive(gens):
            gens = [g for g in gens if g is not None]
            while gens:
                for g in list(gens):
                    try:
                        next(g)
                    except StopIteration:
                        gens.remove(g)

        prev_seq = None
        if "delta" not in SKIP:
            for d in range(NDBL):
                sf = gen_SF(d, d % 2)
                drive([prev_seq, sf])
                prev_seq = gen_SEQ(d, d % 2)
            drive([prev_seq])

        if ti == 0:
            dump("d_og", gate, "gate")
            if KDUMP:
                O("gpsimd", (lambda e: e.dma_start(dbg["d_ST"][:, 0:512], ST[:, :])),
                  r=("ST",), w=("dbgdone",), dma=True, fam="dbgg")
        stream_mm("woS", DM, DM, pmap4(), lambda j: blk(gate, j), lambda j: ("gate", j),
                  epi_copy(vb, "vb"))
        if ti == 0:
            dump("d_ao", vb, "vb")
        ln_stats(vb, "vb")
        ln_apply(vb, "vb", xn, "xn", resid_t=xbf, resid_res="xbf")
        if ti == 0:
            dump("d_ybf", xn, "xn")

        ln_stats(xn, "xn")
        ln_apply(xn, "xn", qn, "qn")
        if ti == 0:
            dump("d_yn", qn, "qn")

        mov_yn = lambda j: blk(qn, j)
        res_yn = lambda j: ("qn", j)
        nob = DF // 128
        for pi, og0 in enumerate(range(0, nob, 2)):
            slots = []
            for wS in ("w1S", "w2S"):
                slot = wslot[wsl[0] % 3]
                sres = f"wslot{wsl[0] % 3}"
                wsl[0] += 1
                slots.append((slot, sres))
                O("sync", (lambda e, og0=og0, slot=slot, wS=wS:
                           e.dma_start(slot[:, 0:2048].rearrange("p (jj o) -> p jj o", jj=8),
                                       scr[wS][0:1024, 128 * og0:128 * og0 + 256]
                                       .rearrange("(jj p) o -> p jj o", p=128))),
                  r=tuple((wS, j) for j in range(8)), w=(sres,), dma=True, fam=sres)
            for wi, pmaps in ((0, PSETS[pi % 3]), (1, PSETS[(pi + 1) % 3])):
                slot, sres = slots[wi]
                for j in range(8):
                    for oi in range(2):
                        pap, bres = pmaps[oi]
                        stat = slot[:, j * 256 + oi * 128: j * 256 + (oi + 1) * 128]
                        O("tensor", (lambda e, pap=pap, stat=stat, mv=mov_yn(j),
                                     first=(j == 0), last=(j == 7):
                                     e.matmul(pap, stat, mv, start=first, stop=last)),
                          r=(sres, res_yn(j)), w=bres)
            for oi in range(2):
                o = og0 + oi
                papA, brA = PSETS[pi % 3][oi]
                papC, brC = PSETS[(pi + 1) % 3][oi]
                uap, ures = u1map[o]
                sq = (sqscr, sqscr2)[pi % 2]
                sqr = ("sqscr", "sqscr2")[pi % 2]
                O("scalar", (lambda e, o=o, papA=papA, sq=sq: e.activation(
                    sq[:, :], papA, AF.Silu, scale=S["s1s"][:, o:o + 1])),
                  r=brA + ("s1s",), w=(sqr,))
                O("vector", (lambda e, o=o, papC=papC, uap=uap, sq=sq: e.scalar_tensor_tensor(
                    uap, papC, S["s2s"][:, o:o + 1], sq[:, :], ALU.mult, ALU.mult)),
                  r=brC + ("s2s", sqr), w=(ures,))

        def epi_gelu(dmap, scal):
            def f(o, pap, bres):
                dap, dres = dmap[o]
                O("scalar", (lambda e, o=o, pap=pap, dap=dap: e.activation(
                    dap, pap, AF.Gelu, scale=S[scal][:, o:o + 1])),
                  r=bres + (scal,), w=(dres,))
            return f

        stream_mm("h1S", DF, DF, pmap4(), lambda j: u1map[j][0], lambda j: u1map[j][1],
                  epi_gelu(h2map, "hs1"))
        stream_mm("h2S", DF, DF, pmap4(), lambda j: h2map[j][0], lambda j: h2map[j][1],
                  epi_gelu(u1map, "hs2"))

        def epi_w3(o, pap, bres):
            O("scalar", (lambda e, o=o, pap=pap: e.mul(blk(qg, o), pap, S["w3s"][:, o:o + 1])),
              r=bres + ("w3s",), w=(("qg", o),))

        stream_mm("w3S", DF, DM, pmap4(), lambda j: u1map[j][0], lambda j: u1map[j][1],
                  epi_w3)

        ln_stats(qg, "qg")
        for b in range(8):
            ob = outb[b % 2]
            obres = f"outb{b % 2}"
            O("vector", (lambda e, b=b: e.tensor_sub(sqscr2[:, :], blk(qg, b), mbs[:, :])),
              r=(("qg", b), "mbs"), w=("sqscr2",))
            O("vector", (lambda e, b=b: e.tensor_mul(sqscr2[:, :], sqscr2[:, :], rbs[:, :])),
              r=("sqscr2", "rbs"), w=("sqscr2",))
            O("vector", (lambda e, b=b, ob=ob: e.tensor_add(ob[:, :], sqscr2[:, :],
                                                            blk(xn, b))),
              r=("sqscr2", ("xn", b)), w=(obres,))
            O("sync", (lambda e, b=b, ob=ob, ts=ts: e.dma_start(
                yt[128 * b:128 * b + 128, ts:ts + CT], ob[:, :])),
              r=(obres,), w=(("ydone", b % 2),), dma=True, fam=f"yo{b % 2}")

    O("sync", lambda e: e.nop(),
      r=(("ydone", 0), ("ydone", 1)) + (("dbgdone",) if KDUMP else ()))

    with nc.Block() as block:
        em.finalize(nc, block)
    return nc


_cache = {}


def prep_inputs(inputs, CT=512):
    cc = host_constants()
    base = {}
    for nm, src in [("wqT", "L0_attn_Wq"), ("wkT", "L0_attn_Wk"), ("wvT", "L0_attn_Wv"),
                    ("wgT", "L0_attn_Wg"), ("woT", "L0_attn_Wo"),
                    ("w1T", "L0_ffn_w1_w"), ("w2T", "L0_ffn_w2_w"),
                    ("h1T", "L0_ffn_h1_w"), ("h2T", "L0_ffn_h2_w"),
                    ("w3T", "L0_ffn_w3_w")]:
        base[nm] = np.ascontiguousarray(np.asarray(inputs[src], np.float32).T)
    base["wab"] = np.ascontiguousarray(np.concatenate(
        [np.asarray(inputs["L0_attn_Wa"], np.float32).T,
         np.asarray(inputs["L0_attn_Wb"], np.float32).T], axis=1))
    for nm, src in [("s1s", "L0_ffn_w1_s"), ("s2s", "L0_ffn_w2_s"),
                    ("hs1", "L0_ffn_h1_s"), ("hs2", "L0_ffn_h2_s")]:
        base[nm] = np.ascontiguousarray(np.asarray(inputs[src], np.float32).reshape(32, 128).T)
    base["w3s"] = np.ascontiguousarray(np.asarray(inputs["L0_ffn_w3_s"], np.float32).reshape(8, 128).T)
    base["bacol"] = np.ascontiguousarray(np.broadcast_to(
        np.asarray(inputs["L0_attn_ba"], np.float32).reshape(16, 1), (16, CT)))
    base["bbcol"] = np.ascontiguousarray(np.broadcast_to(
        np.asarray(inputs["L0_attn_bb"], np.float32).reshape(16, 1), (16, CT)))
    for nm in BF_CONSTS + F32_CONSTS:
        base[nm] = cc[nm]
    return base


def kernel(**inputs):
    T, CT = 8192, 512
    if "prog" not in _cache:
        _cache["prog"] = build_program(T, CT)
    nc = _cache["prog"]
    base = prep_inputs(inputs)
    x = np.asarray(inputs["L0_attn_x"], np.float32)
    in_maps = []
    for c in range(NCORES):
        m = dict(base)
        m["xt"] = np.ascontiguousarray(x[c].T)
        in_maps.append(m)
    res = run_bass_kernel_spmd(nc, in_maps, core_ids=list(range(NCORES)))
    out = np.stack([np.asarray(r["yt"]).T for r in res.results], axis=0)
    return out.astype(np.float32)
